# version 02 (final)
# baseline (speedup 1.0000x reference)
import math
import contextlib
import numpy as np
import ml_dtypes

import concourse.bass as bass
import concourse.mybir as mybir
from concourse.bass_utils import run_bass_kernel_spmd

F32 = mybir.dt.float32
BF16 = mybir.dt.bfloat16
AF = mybir.ActivationFunctionType
ALU = mybir.AluOpType
AX = mybir.AxisListType
BFD = ml_dtypes.bfloat16

T = 4096
D = 1024
NCH = 8
HEADS = 16
HD = 64
WIN = 256
DI = 2730
DIP = 2816
NIC = 22
W1C = 2 * DIP
STRIPE = 512
NS = T // STRIPE
NTS = 4
ZCOLS = 4104
LAT = 128
ROPE = 32
NEG_BIG = -1e30

ENGS = ("y", "t", "v", "s", "p")
DBG_TOKVAL = [False]


def _f32(x):
    return np.ascontiguousarray(x, dtype=np.float32)


def host_tables():
    half = np.arange(0, HD, 2, dtype=np.float64)
    inv_freq = 1.0 / 10000.0 ** (half / HD)
    t = np.arange(T, dtype=np.float64)
    freqs = t[:, None] * inv_freq[None, :]
    freqs = np.concatenate([freqs, freqs], -1)
    base_scale = (half + 0.4 * HD) / (1.4 * HD)
    power = (t - T // 2) / (WIN // 2)
    scale = base_scale[None, :] ** power[:, None]
    scale = np.concatenate([scale, scale], -1)
    cos, sin = np.cos(freqs), np.sin(freqs)
    sgn = np.concatenate([-np.ones(32), np.ones(32)])[None, :]
    c1 = cos * scale
    c2 = sin * scale * sgn
    c3 = cos / scale
    c4 = sin / scale * sgn
    out = {}
    for name, arr in [("c1", c1), ("c2s", c2), ("c3", c3), ("c4s", c4)]:
        rep = np.repeat(arr[:, None, :], 8, axis=1)
        out[name] = rep.astype(BFD)
    return out


class Loop:
    def __init__(self, k, n, tag):
        self.k, self.n, self.tag = k, n, tag
        self.cm = None
        self.iv = None

    def __enter__(self):
        k = self.k
        k.barrier()
        self.e0 = k._e_count
        self.g0 = k._g_count
        k._wait_cache.clear()
        k._loop_stack.append(self)
        eng = k.eng[k.cur]
        self.cm = eng.Fori(0, self.n)
        self.iv = self.cm.__enter__()
        return self

    @property
    def i(self):
        return self.iv

    def __exit__(self, *a):
        k = self.k
        k._emit_loop_barrier(self)
        self.cm.__exit__(*a)
        k._loop_stack.pop()
        k._e_count = self.e0 + 5 * self.n
        k._g_count = self.g0 + self.n
        k._wait_cache.clear()
        k._epoch += 1
        for s in list(k.n):
            k.n[s] = 0
        return False


class K:

    def __init__(self, nc):
        self.nc = nc
        self.eng = {"y": nc.sync, "t": nc.tensor, "v": nc.vector,
                    "s": nc.scalar, "p": nc.gpsimd}
        self.cur = None
        self.sem = {}
        self.n = {}
        self.res = {}
        self.readers = {}
        self._wait_cache = {}
        self._epoch = 0
        self._loop_stack = []
        self._barrier_no = 0
        self._exit_stack = None

    DMA_CLS = ("dy", "ds", "dz", "dw", "dga", "dgb", "dgc")

    def alloc_sems(self, ctx):
        for name in ("wt", "wv", "ws", "wp", "wy", "send", "sgo") + self.DMA_CLS:
            self.sem[name] = ctx.enter_context(self.nc.semaphore(name))
            self.n[name] = 0
        del self.n["send"], self.n["sgo"]

    ENG_SEM = {"t": "wt", "v": "wv", "s": "ws", "p": "wp", "y": "wy"}
    DMA_SEM = {"y": "dy", "s": "ds"}

    def _wait(self, sem_name, value):
        if value <= 0:
            return
        if self._wait_cache.get(sem_name, -1) >= value:
            return
        self._wait_cache[sem_name] = value
        self.eng[self.cur].wait_ge(self.sem[sem_name], value)

    def _dep_waits(self, reads, writes):
        def live(tok):
            return tok is not None and tok[2] == self._epoch and tok[1] <= self.n[tok[0]]

        def val(tok):
            return tok[1]
        for r in tuple(reads) + tuple(writes):
            tok = self.res.get(r)
            if live(tok):
                self._wait(tok[0], val(tok))
        for w in writes:
            for tok in self.readers.get(w, ()):
                if live(tok):
                    self._wait(tok[0], val(tok))

    def _register(self, reads, writes, token):
        for w in writes:
            self.res[w] = token
            self.readers[w] = []
        for r in reads:
            self.readers.setdefault(r, []).append(token)

    def op(self, engine, fn, reads=(), writes=()):
        sem_name = self.ENG_SEM[engine]
        if self.cur == engine:
            self._dep_waits(reads, writes)
        self.n[sem_name] += 1
        token = (sem_name, self.n[sem_name], self._epoch)
        if self.cur == engine:
            inst = fn(self.eng[engine])
            inst.then_inc(self.sem[sem_name], 1)
        self._register(reads, writes, token)
        return token

    def ops(self, engine, fns, reads=(), writes=()):
        sem_name = self.ENG_SEM[engine]
        if self.cur == engine:
            self._dep_waits(reads, writes)
        self.n[sem_name] += 1
        token = (sem_name, self.n[sem_name], self._epoch)
        if self.cur == engine:
            eng = self.eng[engine]
            inst = None
            for fn in fns:
                inst = fn(eng)
            inst.then_inc(self.sem[sem_name], 1)
        self._register(reads, writes, token)
        return token

    def dma(self, queue, out, in_, reads=(), writes=(), slow=False, cls=None):
        sem_name = cls if cls is not None else self.DMA_SEM[queue]
        if self.cur == queue:
            self._dep_waits(reads, writes)
        self.n[sem_name] += 16
        token = (sem_name, self.n[sem_name], self._epoch)
        if self.cur == queue:
            self._wait(sem_name, token[1] - 16)
            o = out(self) if callable(out) else out
            i = in_(self) if callable(in_) else in_
            kw = dict(allow_slow_non_contiguous=True) if slow else {}
            self.eng[queue].dma_start(out=o, in_=i, **kw).then_inc(self.sem[sem_name], 16)
        self._register(reads, writes, token)
        return token

    def loop(self, n, tag=""):
        return Loop(self, n, tag)

    @property
    def i(self):
        return self._loop_stack[-1].iv

    def _drain_dmas(self):
        for sem_name in self.DMA_CLS:
            q = "s" if sem_name == "ds" else "y"
            if self.cur == q and self.n[sem_name] > 0:
                self._wait(sem_name, self.n[sem_name])
        own = self.ENG_SEM[self.cur]
        if self.n[own] > 0:
            self._wait(own, self.n[own])

    def _emit_loop_barrier(self, loop):
        k = self
        eng = k.eng[k.cur]
        send, sgo = k.sem["send"], k.sem["sgo"]
        self._drain_dmas()
        eng.sem_inc(send, 1)
        eng.wait_ge(send, loop.iv * 5 + (loop.e0 + 5))
        if k.cur == "p":
            for name in ("wt", "wv", "ws", "wp", "wy") + K.DMA_CLS:
                eng.sem_clear(k.sem[name])
            eng.sem_inc(sgo, 1)
        eng.wait_ge(sgo, loop.iv * 1 + (loop.g0 + 1))

    def barrier(self):
        eng = self.eng[self.cur]
        send, sgo = self.sem["send"], self.sem["sgo"]
        self._drain_dmas()
        self._e_count += 5
        self._g_count += 1
        eng.sem_inc(send, 1)
        eng.wait_ge(send, self._e_count)
        if self.cur == "p":
            for name in ("wt", "wv", "ws", "wp", "wy") + K.DMA_CLS:
                eng.sem_clear(self.sem[name])
            eng.sem_inc(sgo, 1)
        eng.wait_ge(sgo, self._g_count)
        self._wait_cache.clear()
        self._epoch += 1
        for s in list(self.n):
            self.n[s] = 0


class Prog:
    pass


def emit_all(k: K, P: Prog, parts):
    if "precast" in parts:
        emit_precast(k, P)
        k.barrier()
    if "stagein" in parts:
        emit_stage_in(k, P)
        k.barrier()
    if "flayers1" in parts:
        emit_f_call(k, P, 0)
    if "mla" in parts:
        emit_stage_y1(k, P)
        k.barrier()
        emit_mla(k, P)
        k.barrier()
        emit_stage_z2(k, P)
        k.barrier()
    if "flayers2" in parts:
        emit_f_call(k, P, 1)
    if "stageout" in parts:
        emit_stage_out(k, P)
        k.barrier()




def build(parts, debug_dram=(), n_layers=6):
    nc = bass.Bass("TRN2", target_bir_lowering=False)
    P = Prog()
    P.n_layers = n_layers
    declare_dram(nc, P, debug_dram)
    with contextlib.ExitStack() as ctx:
        k = K(nc)
        k.alloc_sems(ctx)
        alloc_tiles(nc, P, ctx)
        block = ctx.enter_context(nc.Block())

        def run(engine_name):
            def body(eng):
                k.cur = engine_name
                k._wait_cache = {}
                k._epoch = 0
                for s in list(k.n):
                    k.n[s] = 0
                k._e_count = 0
                k._g_count = 0
                k.res = {}
                k.readers = {}
                emit_all(k, P, parts)
            return body

        block.sync(run("y"))
        block.tensor(run("t"))
        block.vector(run("v"))
        block.scalar(run("s"))
        block.gpsimd(run("p"))
    return nc, P




def declare_dram(nc, P, debug_dram=()):
    def dt(name, shape, dtype, kind):
        k = "ExternalOutput" if name in debug_dram else kind
        return nc.dram_tensor(name, shape, dtype, kind=k).ap()

    P.x_in = dt("L0_attn_x", [T, D], F32, "ExternalInput")
    P.awq = dt("L0_attn_Wq", [D, D], F32, "ExternalInput")
    P.awkv = dt("L0_attn_Wkv", [D, LAT], F32, "ExternalInput")
    P.awk = dt("L0_attn_Wk", [LAT, 512], F32, "ExternalInput")
    P.awv = dt("L0_attn_Wv", [LAT, 512], F32, "ExternalInput")
    P.awo = dt("L0_attn_Wo", [D, D], F32, "ExternalInput")
    P.aqg = dt("L0_attn_q_g", [HD], F32, "ExternalInput")
    P.akg = dt("L0_attn_k_g", [ROPE], F32, "ExternalInput")
    P.fwqkv = dt("L0_ffn_Wqkv", [6, D, 3 * D], F32, "ExternalInput")
    P.fqs = dt("L0_ffn_q_scale", [6, HD], F32, "ExternalInput")
    P.fks = dt("L0_ffn_k_scale", [6, HD], F32, "ExternalInput")
    P.fwo = dt("L0_ffn_Wo", [6, D, D], F32, "ExternalInput")
    P.flng = dt("L0_ffn_ln_g", [6, D], F32, "ExternalInput")
    P.flnb = dt("L0_ffn_ln_b", [6, D], F32, "ExternalInput")
    P.fw1 = dt("L0_ffn_W1", [6, D, 2 * DI], F32, "ExternalInput")
    P.fb1 = dt("L0_ffn_b1", [6, 2 * DI], F32, "ExternalInput")
    P.fw2 = dt("L0_ffn_W2", [6, DI, D], F32, "ExternalInput")
    P.fb2 = dt("L0_ffn_b2", [6, D], F32, "ExternalInput")
    P.c1 = dt("tab_c1", [T, 8, HD], BF16, "ExternalInput")
    P.c2s = dt("tab_c2s", [T, 8, HD], BF16, "ExternalInput")
    P.c3 = dt("tab_c3", [T, 8, HD], BF16, "ExternalInput")
    P.c4s = dt("tab_c4s", [T, 8, HD], BF16, "ExternalInput")
    P.y_out = dt("out", [T, D], F32, "ExternalOutput")
    P.z_hbm = dt("z_hbm", [NCH, 128, ZCOLS], F32, "Internal")
    P.y_hbm = dt("y_hbm", [NCH, 128, ZCOLS], F32, "Internal")
    P.x_hbm = dt("x_hbm", [NCH, 128, ZCOLS], F32, "Internal")
    P.rstd_hbm = dt("rstd_hbm", [2, T], F32, "Internal")
    P.ytok_hbm = dt("ytok_hbm", [T, D], F32, "Internal")
    P.y2tok_hbm = dt("y2tok_hbm", [T, D], F32, "Internal")
    P.SEC = SEC = {}
    P.SECSZ = SECSZ = {}
    off = 0
    def sec(name, n):
        nonlocal off
        SEC[name] = off
        SECSZ[name] = (n + 2047) // 2048 * 2048
        off += (n + 2047) // 2048 * 2048
    sec("qkv", NCH * 128 * 3 * D)
    sec("wo", NCH * 128 * D)
    sec("w1", NCH * 128 * W1C)
    sec("w1aug", 2 * W1C)
    sec("w2", NIC * 128 * D)
    sec("w2aug", D)
    sec("qsrep", 128 * 512)
    sec("ksrep", 128 * 512)
    P.SLAB_E = off
    P.wslab = dt("wslab", [6, P.SLAB_E], BF16, "Internal")
    P.stage = dt("stage", [P.SLAB_E], BF16, "Internal")
    P.wq_b = dt("wq_b", [NCH, 128, D], BF16, "Internal")
    P.wqaug_b = dt("wqaug_b", [1, D], BF16, "Internal")
    P.wkv_b = dt("wkv_b", [NCH, 128, LAT], BF16, "Internal")
    P.wkvaug_b = dt("wkvaug_b", [1, LAT], BF16, "Internal")
    P.wk_b = dt("wk_b", [LAT, 512], BF16, "Internal")
    P.wv_b = dt("wv_b", [LAT, 512], BF16, "Internal")
    P.womla_b = dt("womla_b", [4, 128, D], BF16, "Internal")
    P.tap_hbm = dt("tap_hbm", [128, 57344], BF16, "Internal")
    P.onesrow_f = nc.inline_tensor(np.ones((1, 512), np.float32), "onesrow_f").ap()
    P.onesrow_b = nc.inline_tensor(np.ones((1, 512), BFD), "onesrow_b").ap()


def alloc_tiles(nc, P, ctx):
    sb = lambda name, shape, dtype=BF16: ctx.enter_context(nc.sbuf_tensor(name, shape, dtype))
    P.ident = sb("ident", [128, 128], BF16)
    P.ones1 = sb("ones1", [1, 128], BF16)
    P.onescol = sb("onescol", [128, 1], BF16)
    P.onescol_f = sb("onescol_f", [128, 1], F32)
    P.maskU2 = sb("maskU2", [128, 256], BF16)
    P.maskL2 = sb("maskL2", [128, 256], BF16)
    P.stats = sb("stats", [1, 4 * 512], F32)
    P.rstdb = sb("rstdb", [128, 512], BF16)
    P.mur2 = sb("mur2", [2, 512], F32)
    P.identf = sb("identf", [128, 128], F32)
    P.cbias = sb("cbias", [128, 2], F32)
    P.ps0 = ctx.enter_context(nc.psum_tensor("ps0", [128, 512], F32))
    P.ps1 = ctx.enter_context(nc.psum_tensor("ps1", [128, 512], F32))
    P.ps23 = ctx.enter_context(nc.psum_tensor("ps23", [128, 1024], F32))
    P.ps45 = ctx.enter_context(nc.psum_tensor("ps45", [128, 1024], F32))
    P.ps6 = ctx.enter_context(nc.psum_tensor("ps6", [128, 512], F32))
    P.psb = ctx.enter_context(nc.psum_tensor("psb", [128, 1024], BF16))

    with contextlib.ExitStack() as pc:
        sbp = lambda name, shape, dtype: pc.enter_context(nc.sbuf_tensor(name, shape, dtype))
        P.pc_wf = [sbp(f"pc_wf{b}", [128, 5460], F32) for b in range(2)]
        P.pc_wb = [sbp(f"pc_wb{b}", [128, 5460], BF16) for b in range(2)]
        P.pc_wp = [sbp(f"pc_wp{b}", [128, W1C], BF16) for b in range(2)]
        P.pc_rowg = sbp("pc_rowg", [1, W1C], F32)
        P.pc_rowb1 = sbp("pc_rowb1", [1, W1C], F32)
        P.pc_rowx = sbp("pc_rowx", [1, W1C], F32)
        P.pc_rb0 = sbp("pc_rb0", [1, W1C], BF16)
        P.pc_rb1 = sbp("pc_rb1", [1, W1C], BF16)
        P.pc_col = sbp("pc_col", [128, 2], F32)
        P.pc_colb = sbp("pc_colb", [128, 2], BF16)

    with contextlib.ExitStack() as fp:
        sbf = lambda name, shape, dtype: fp.enter_context(nc.sbuf_tensor(name, shape, dtype))
        SC = 520
        P.zin = [sbf(f"zin{b}", [128, NCH * SC], F32) for b in range(2)]
        P.SC = SC
        P.x1 = sbf("x1", [128, NCH * SC], BF16)
        P.qhat = sbf("qhat", [128, NCH * 512], BF16)
        P.khat = sbf("khat", [128, NCH * 768], BF16)
        P.vring = sbf("vring", [128, 6 * 1024], BF16)
        P.ao = sbf("ao", [128, NCH * 512], BF16)
        P.wbuf = [sbf(f"wbuf{b}", [128, 8 * 512], BF16) for b in range(2)]
        P.w2buf = [sbf(f"w2buf{b}", [128, NIC * 128], BF16) for b in range(2)]
        P.w1augt = sbf("w1augt", [2, W1C], BF16)
        P.b2row = sbf("b2row", [1, D], BF16)
        P.crep = sbf("crep", [128, NTS * 2048], BF16)
        P.qsrep = sbf("qsrep", [128, 512], BF16)
        P.ksrep = sbf("ksrep", [128, 512], BF16)
        P.sq = sbf("sq", [128, 512], BF16)
        P.u_t = sbf("u_t", [128, 512], F32)
        P.t1 = sbf("t1", [128, 512], F32)
        P.t2 = sbf("t2", [128, 512], F32)
        P.urf = [sbf(f"urf{b}", [128, 512], F32) for b in range(2)]
        P.urot = [sbf(f"urot{b}", [128, 512], BF16) for b in range(2)]
        P.nrm = [sbf(f"nrm{b}", [128, 8], F32) for b in range(2)]
        P.rk = [sbf(f"rk{b}", [128, 8], F32) for b in range(2)]
        P.P_t = [sbf(f"P_t{b}", [128, 2 * 384], BF16) for b in range(2)]
        P.Pt_sb = [sbf(f"Pt_sb{b}", [128, 2 * 384], BF16) for b in range(2)]
        P.negm = [sbf(f"negm{b}", [128, 2], F32) for b in range(2)]
        P.nb = [sbf(f"nb{b}", [128, 2], F32) for b in range(2)]
        P.sums = [sbf(f"sums{b}", [128, 2], F32) for b in range(2)]
        P.rsum = [sbf(f"rsum{b}", [128, 2], F32) for b in range(2)]
        P.diag2 = [sbf(f"diag2{b}", [128, 2 * 128], BF16) for b in range(2)]
        P.xn = sbf("xn", [128, NCH * 512], BF16)
        P.xnaug = sbf("xnaug", [2, 512], BF16)
        P.h_t = sbf("h_t", [128, NIC * 512], BF16)
        P.haug = sbf("haug", [1, 512], BF16)
        P.gl = sbf("gl", [128, 512], BF16)
        P.rstdrow = sbf("rstdrow", [1, 512], BF16)
        P.zcarry = sbf("zcarry", [128, 8], F32)
        P.stcol = sbf("stcol", [128, 8], F32)
        P.stin = None

    with contextlib.ExitStack() as mp:
        sbm = lambda name, shape, dtype: mp.enter_context(nc.sbuf_tensor(name, shape, dtype))
        P.m_khat = sbm("m_khat", [128, 4 * T], BF16)
        P.m_qhat = sbm("m_qhat", [128, 4 * 512], BF16)
        P.m_v = sbm("m_v", [128, 32 * 512], BF16)
        P.m_kv = sbm("m_kv", [128, T], BF16)
        P.m_P = sbm("m_P", [128, 2 * T], BF16)
        P.m_Pt = sbm("m_Pt", [128, 2 * 1024], BF16)
        P.m_y = sbm("m_y", [128, NCH * 512], F32)
        P.m_xa = sbm("m_xa", [128, NCH * 512], BF16)
        P.m_ytok = sbm("m_ytok", [128, D], F32)
        P.m_y2 = sbm("m_y2", [128, D], F32)
        P.m_wq = sbm("m_wq", [128, 8 * 1024], BF16)
        P.m_wkv = sbm("m_wkv", [128, 8 * 128], BF16)
        P.m_wk = sbm("m_wk", [128, 512], BF16)
        P.m_wv = sbm("m_wv", [128, 512], BF16)
        P.m_wo = sbm("m_wo", [128, 4 * 1024], BF16)
        P.m_aug = sbm("m_aug", [1, 1024], BF16)
        P.m_augkv = sbm("m_augkv", [1, 128], BF16)
        P.m_sq = sbm("m_sq", [128, 1024], BF16)
        P.m_u = sbm("m_u", [128, 512], F32)
        P.m_r16 = sbm("m_r16", [128, 16], F32)
        P.m_qgrep = sbm("m_qgrep", [128, 512], BF16)
        P.m_kgrep = sbm("m_kgrep", [128, 512], BF16)
        P.m_grow = sbm("m_grow", [1, 512], F32)
        P.m_growb = sbm("m_growb", [1, 512], BF16)
        P.m_rstdf = sbm("m_rstdf", [1, 512], F32)
        P.m_murf = sbm("m_murf", [1, 512], F32)
        P.m_rstd_b = sbm("m_rstd_b", [1, 512], BF16)
        P.m_mur_b = sbm("m_mur_b", [1, 512], BF16)
        P.m_ao = sbm("m_ao", [128, 512], BF16)
        P.m_negm = sbm("m_negm", [128, 2], F32)
        P.m_nb = sbm("m_nb", [128, 2], F32)
        P.m_sums = sbm("m_sums", [128, 2], F32)
        P.m_rsum = sbm("m_rsum", [128, 2], F32)
        P.m_diag2 = sbm("m_diag2", [128, 2 * 128], BF16)
        P.m_cmax = sbm("m_cmax", [128, 2], F32)


def emit_const_setup(k: K, P: Prog):
    k.op("p", lambda e: e.memset(P.ident[:], 1.0), writes=["ident"])
    k.op("p", lambda e: e.affine_select(out=P.ident[:], in_=P.ident[:],
                                        pattern=[[-1, 128]], compare_op=ALU.is_equal,
                                        fill=0.0, base=0, channel_multiplier=1),
         reads=["ident"], writes=["ident"])
    k.op("p", lambda e: e.memset(P.identf[:], 1.0), writes=["identf"])
    k.op("p", lambda e: e.affine_select(out=P.identf[:], in_=P.identf[:],
                                        pattern=[[-1, 128]], compare_op=ALU.is_equal,
                                        fill=0.0, base=0, channel_multiplier=1),
         reads=["identf"], writes=["identf"])
    k.op("p", lambda e: e.memset(P.ones1[:], 1.0), writes=["ones1"])
    k.op("p", lambda e: e.memset(P.onescol[:], 1.0), writes=["onescol"])
    k.op("p", lambda e: e.memset(P.onescol_f[:], 1.0), writes=["onescol_f"])
    k.op("p", lambda e: e.memset(P.cbias[:, 0:1], 1e-5), writes=["cbias"])
    k.op("p", lambda e: e.memset(P.cbias[:, 1:2], 1e-6), writes=["cbias"])
    k.op("p", lambda e: e.memset(P.maskU2[:], 0.0), writes=["maskU"])
    k.op("p", lambda e: e.affine_select(out=P.maskU2[:, 0:128], in_=P.maskU2[:, 0:128],
                                        pattern=[[1, 128]], compare_op=ALU.is_ge,
                                        fill=NEG_BIG, base=0, channel_multiplier=-1),
         reads=["maskU"], writes=["maskU"])
    k.op("p", lambda e: e.tensor_copy(out=P.maskU2[:, 128:256], in_=P.maskU2[:, 0:128]),
         reads=["maskU"], writes=["maskU"])
    k.op("p", lambda e: e.memset(P.maskL2[:], 0.0), writes=["maskL"])
    k.op("p", lambda e: e.affine_select(out=P.maskL2[:, 0:128], in_=P.maskL2[:, 0:128],
                                        pattern=[[-1, 128]], compare_op=ALU.is_ge,
                                        fill=NEG_BIG, base=0, channel_multiplier=1),
         reads=["maskL"], writes=["maskL"])
    k.op("p", lambda e: e.tensor_copy(out=P.maskL2[:, 128:256], in_=P.maskL2[:, 0:128]),
         reads=["maskL"], writes=["maskL"])
    k.dma("y", lambda kk: bass.AP(P.mur2, 512, [[512, 1], [1, 512]]),
          P.onesrow_f, writes=["mur2r1"])
    k.dma("y", lambda kk: P.haug[0:1, :], P.onesrow_b, writes=["haug"])


def _row_accum_mms(k, P, lhs_tile, lhs_col, rhs_tile, width, dst_row_tile, dst_row,
                   lhs_res, rhs_res, row_res, n_valid=None):
    n_valid = width if n_valid is None else n_valid
    nchunks = (n_valid + 511) // 512
    for j in range(nchunks):
        n = min(512, n_valid - j * 512)
        pname = "ps0" if j % 2 == 0 else "ps1"
        ps = P.ps0 if j % 2 == 0 else P.ps1
        k.op("t", lambda e, j=j, n=n, ps=ps: e.matmul(
            out=ps.ap()[0:1, 0:n],
            lhsT=lhs_tile[:, lhs_col:lhs_col + 1],
            rhs=rhs_tile[:, j * 512: j * 512 + n],
            start=True, stop=True),
            reads=[lhs_res, rhs_res], writes=[pname])
        k.op("v", lambda e, j=j, n=n, ps=ps: e.tensor_tensor(
            out=dst_row_tile[dst_row:dst_row + 1, j * 512: j * 512 + n],
            in0=dst_row_tile[dst_row:dst_row + 1, j * 512: j * 512 + n],
            in1=ps.ap()[0:1, 0:n], op=ALU.add),
            reads=[pname, row_res], writes=[pname, row_res])


def slab_w(P, l, name, extra, pattern):
    return bass.AP(P.wslab.tensor, l * P.SLAB_E + P.SEC[name] + extra, pattern)


def stage_r(P, name, extra, pattern):
    return bass.AP(P.stage.tensor, P.SEC[name] + extra, pattern)


def _cast_store(k, P, src_fn, dst_fn, width, sl, qdma="y"):
    wf, wb = f"pc_wf{sl}", f"pc_wb{sl}"
    k.dma(qdma, lambda kk: P.pc_wf[sl][:, 0:width], src_fn, writes=[wf])
    k.op("p", lambda e: e.tensor_copy(out=P.pc_wb[sl][:, 0:width], in_=P.pc_wf[sl][:, 0:width]),
         reads=[wf], writes=[wb])
    k.dma(qdma, dst_fn, lambda kk: P.pc_wb[sl][:, 0:width], reads=[wb])


def emit_precast(k: K, P: Prog):
    emit_const_setup(k, P)
    for l in range(P.n_layers):
        for kc in range(NCH):
            _cast_store(k, P,
                        P.fwqkv[l, kc * 128:(kc + 1) * 128, :],
                        slab_w(P, l, "qkv", kc * 128 * 3 * D, [[3 * D, 128], [1, 3 * D]]),
                        3 * D, kc % 2)
        for kc in range(NCH):
            _cast_store(k, P,
                        P.fwo[l, kc * 128:(kc + 1) * 128, :],
                        slab_w(P, l, "wo", kc * 128 * D, [[D, 128], [1, D]]),
                        D, kc % 2)
        for nm, srcv in (("qsrep", P.fqs), ("ksrep", P.fks)):
            k.dma("s", lambda kk: P.pc_rowx[0:1, 0:HD],
                  srcv[l, :].unsqueeze(0), writes=["row3"])
            k.op("v", lambda e: e.tensor_copy(out=P.pc_rb0[0:1, 0:HD],
                                              in_=P.pc_rowx[0:1, 0:HD]),
                 reads=["row3"], writes=["rowb"])
            k.op("t", lambda e: e.matmul(out=P.ps0.ap()[:, 0:HD], lhsT=P.ones1[:, :],
                                         rhs=P.pc_rb0[0:1, 0:HD], start=True, stop=True),
                 reads=["rowb", "ones1"], writes=["ps0"])
            k.op("v", lambda e: e.tensor_copy(out=P.pc_wb[0][:, 0:HD], in_=P.ps0.ap()[:, 0:HD]),
                 reads=["ps0"], writes=["ps0", "pc_wb0"])
            for j in range(1, 8):
                k.op("v", lambda e, j=j: e.tensor_copy(out=P.pc_wb[0][:, j * HD:(j + 1) * HD],
                                                       in_=P.pc_wb[0][:, 0:HD]),
                     reads=["pc_wb0"], writes=["pc_wb0"])
            k.dma("y", slab_w(P, l, nm, 0, [[512, 128], [1, 512]]),
                  lambda kk: P.pc_wb[0][:, 0:512],
                  reads=["pc_wb0"])
        k.op("v", lambda e: e.memset(P.pc_rowg[0:1, :], 0.0), writes=["rows"])
        k.op("v", lambda e: e.memset(P.pc_rowb1[0:1, :], 0.0), writes=["rowsb1"])
        for kc in range(NCH):
            sl = kc % 2
            wf, wb, wp = f"pc_wf{sl}", f"pc_wb{sl}", f"pc_wp{sl}"
            k.dma("y", lambda kk, sl=sl: P.pc_wf[sl][:, 0:2 * DI],
                  P.fw1[l, kc * 128:(kc + 1) * 128, :], writes=[wf])
            k.dma("s", lambda kk: bass.AP(P.pc_col, 0, [[2, 128], [1, 1]]),
                  P.flng[l, kc * 128:(kc + 1) * 128].unsqueeze(1), writes=["pc_col0"])
            k.dma("s", lambda kk: bass.AP(P.pc_col, 1, [[2, 128], [1, 1]]),
                  P.flnb[l, kc * 128:(kc + 1) * 128].unsqueeze(1), writes=["pc_col1"])
            k.op("v", lambda e: e.tensor_copy(out=P.pc_colb[:], in_=P.pc_col[:]),
                 reads=["pc_col0", "pc_col1"], writes=["pc_colb"])
            k.op("p", lambda e, sl=sl: e.tensor_copy(out=P.pc_wb[sl][:, 0:2 * DI],
                                                     in_=P.pc_wf[sl][:, 0:2 * DI]),
                 reads=[wf], writes=[wb])
            _row_accum_mms(k, P, P.pc_colb, 1, P.pc_wb[sl], 2 * DI, P.pc_rowb1, 0,
                           "pc_colb", wb, "rowsb1")
            k.op("p", lambda e, sl=sl: e.memset(
                P.pc_wp[sl][:, 21 * 256 + 42: 21 * 256 + 128], 0.0), writes=[wp + "p1"])
            k.op("p", lambda e, sl=sl: e.memset(
                P.pc_wp[sl][:, 21 * 256 + 128 + 42: 21 * 256 + 256], 0.0), writes=[wp + "p2"])
            k.op("p", lambda e, sl=sl: e.tensor_scalar(
                out=bass.AP(P.pc_wp[sl], 0, [[W1C, 128], [256, 21], [1, 128]]),
                in0=bass.AP(P.pc_wf[sl], 0, [[5460, 128], [128, 21], [1, 128]]),
                scalar1=bass.AP(P.pc_col, 0, [[2, 128], [1, 1]]),
                scalar2=None, op0=ALU.mult),
                reads=[wf, "pc_col0"], writes=[wp + "a"])
            k.op("p", lambda e, sl=sl: e.tensor_scalar(
                out=bass.AP(P.pc_wp[sl], 21 * 256, [[W1C, 128], [1, 42]]),
                in0=bass.AP(P.pc_wf[sl], 21 * 128, [[5460, 128], [1, 42]]),
                scalar1=bass.AP(P.pc_col, 0, [[2, 128], [1, 1]]),
                scalar2=None, op0=ALU.mult),
                reads=[wf, "pc_col0"], writes=[wp + "at"])
            k.op("p", lambda e, sl=sl: e.tensor_scalar(
                out=bass.AP(P.pc_wp[sl], 128, [[W1C, 128], [256, 21], [1, 128]]),
                in0=bass.AP(P.pc_wf[sl], DI, [[5460, 128], [128, 21], [1, 128]]),
                scalar1=bass.AP(P.pc_col, 0, [[2, 128], [1, 1]]),
                scalar2=None, op0=ALU.mult),
                reads=[wf, "pc_col0"], writes=[wp + "g"])
            k.op("p", lambda e, sl=sl: e.tensor_scalar(
                out=bass.AP(P.pc_wp[sl], 21 * 256 + 128, [[W1C, 128], [1, 42]]),
                in0=bass.AP(P.pc_wf[sl], DI + 21 * 128, [[5460, 128], [1, 42]]),
                scalar1=bass.AP(P.pc_col, 0, [[2, 128], [1, 1]]),
                scalar2=None, op0=ALU.mult),
                reads=[wf, "pc_col0"], writes=[wp + "gt"])
            kk_done = [wp + s for s in ("a", "at", "g", "gt", "p1", "p2")]
            for j in range(W1C // 512):
                pname = "ps0" if j % 2 == 0 else "ps1"
                ps = P.ps0 if j % 2 == 0 else P.ps1
                k.op("t", lambda e, j=j, ps=ps, sl=sl: e.matmul(
                    out=ps.ap()[0:1, 0:512],
                    lhsT=P.onescol[:, 0:1],
                    rhs=P.pc_wp[sl][:, j * 512:(j + 1) * 512],
                    start=True, stop=True),
                    reads=kk_done + ["onescol"], writes=[pname])
                k.op("v", lambda e, j=j, ps=ps: e.tensor_tensor(
                    out=P.pc_rowg[0:1, j * 512:(j + 1) * 512],
                    in0=P.pc_rowg[0:1, j * 512:(j + 1) * 512],
                    in1=ps.ap()[0:1, 0:512], op=ALU.add),
                    reads=[pname, "rows"], writes=[pname, "rows"])
            k.dma("y", slab_w(P, l, "w1", kc * 128 * W1C, [[W1C, 128], [1, W1C]]),
                  lambda kk, sl=sl: P.pc_wp[sl][:], reads=kk_done)
        k.dma("s", lambda kk: P.pc_rowx[0:1, 0:2 * DI],
              P.fb1[l, :].unsqueeze(0), writes=["row3"])
        k.op("v", lambda e: e.tensor_tensor(out=P.pc_rowb1[0:1, 0:2 * DI],
                                            in0=P.pc_rowb1[0:1, 0:2 * DI],
                                            in1=P.pc_rowx[0:1, 0:2 * DI], op=ALU.add),
             reads=["rowsb1", "row3"], writes=["rowsb1"])
        k.op("v", lambda e: e.memset(P.pc_rb1[0:1, :], 0.0), reads=[], writes=["row2"])
        k.op("v", lambda e: e.tensor_copy(
            out=bass.AP(P.pc_rb1, 0, [[W1C, 1], [256, 21], [1, 128]]),
            in_=bass.AP(P.pc_rowb1, 0, [[W1C, 1], [128, 21], [1, 128]])),
            reads=["rowsb1", "row2"], writes=["row2"])
        k.op("v", lambda e: e.tensor_copy(
            out=bass.AP(P.pc_rb1, 21 * 256, [[W1C, 1], [1, 42]]),
            in_=bass.AP(P.pc_rowb1, 21 * 128, [[W1C, 1], [1, 42]])),
            reads=["rowsb1"], writes=["row2b"])
        k.op("v", lambda e: e.tensor_copy(
            out=bass.AP(P.pc_rb1, 128, [[W1C, 1], [256, 21], [1, 128]]),
            in_=bass.AP(P.pc_rowb1, DI, [[W1C, 1], [128, 21], [1, 128]])),
            reads=["rowsb1"], writes=["row2c"])
        k.op("v", lambda e: e.tensor_copy(
            out=bass.AP(P.pc_rb1, 21 * 256 + 128, [[W1C, 1], [1, 42]]),
            in_=bass.AP(P.pc_rowb1, DI + 21 * 128, [[W1C, 1], [1, 42]])),
            reads=["rowsb1"], writes=["row2d"])
        k.op("v", lambda e: e.tensor_scalar(out=P.pc_rb0[0:1, :], in0=P.pc_rowg[0:1, :],
                                            scalar1=-1.0, scalar2=None, op0=ALU.mult),
             reads=["rows"], writes=["rowb"])
        k.dma("y", slab_w(P, l, "w1aug", 0, [[1, 1], [1, W1C]]),
              lambda kk: P.pc_rb0[0:1, :], reads=["rowb"])
        k.dma("y", slab_w(P, l, "w1aug", W1C, [[1, 1], [1, W1C]]),
              lambda kk: P.pc_rb1[0:1, :], reads=["row2", "row2b", "row2c", "row2d"])
        for kc in range(NIC):
            r0 = kc * 128
            n_real = min(128, DI - r0)
            sl = kc % 2
            wf, wb = f"pc_wf{sl}", f"pc_wb{sl}"
            if n_real < 128:
                k.op("v", lambda e, sl=sl: e.memset(P.pc_wf[sl][:, 0:D], 0.0), writes=[wf])
            k.dma("y", lambda kk, r0=r0, n_real=n_real, sl=sl: P.pc_wf[sl][0:n_real, 0:D],
                  P.fw2[l, r0:r0 + n_real, :], writes=[wf])
            k.op("p", lambda e, sl=sl: e.tensor_copy(out=P.pc_wb[sl][:, 0:D],
                                                     in_=P.pc_wf[sl][:, 0:D]),
                 reads=[wf], writes=[wb])
            k.dma("y", slab_w(P, l, "w2", kc * 128 * D, [[D, 128], [1, D]]),
                  lambda kk, sl=sl: P.pc_wb[sl][:, 0:D], reads=[wb])
        k.dma("s", lambda kk: P.pc_rowx[0:1, 0:D],
              P.fb2[l, :].unsqueeze(0), writes=["row3"])
        k.op("v", lambda e: e.tensor_copy(out=P.pc_rb0[0:1, 0:D], in_=P.pc_rowx[0:1, 0:D]),
             reads=["row3"], writes=["rowb"])
        k.dma("y", slab_w(P, l, "w2aug", 0, [[D, 1], [1, D]]),
              lambda kk: P.pc_rb0[0:1, 0:D], reads=["rowb"])

    k.op("v", lambda e: e.memset(P.pc_rowg[0:1, :], 0.0), writes=["mrows"])
    for kc in range(NCH):
        sl = kc % 2
        wb = f"pc_wb{sl}"
        _cast_store(k, P, P.awq[kc * 128:(kc + 1) * 128, :],
                    lambda kk, kc=kc: P.wq_b[kc, :, :], D, sl)
        _row_accum_mms(k, P, P.onescol, 0, P.pc_wb[sl], D, P.pc_rowg, 0,
                       "onescol", wb, "mrows")
    k.op("v", lambda e: e.tensor_scalar(out=P.pc_rb0[0:1, 0:D], in0=P.pc_rowg[0:1, 0:D],
                                        scalar1=-1.0, scalar2=None, op0=ALU.mult),
         reads=["mrows"], writes=["mrowb"])
    k.dma("y", lambda kk: P.wqaug_b, lambda kk: P.pc_rb0[0:1, 0:D], reads=["mrowb"])
    k.op("v", lambda e: e.memset(P.pc_rowg[0:1, 0:LAT], 0.0), writes=["mrows"])
    for kc in range(NCH):
        sl = kc % 2
        wb = f"pc_wb{sl}"
        _cast_store(k, P, P.awkv[kc * 128:(kc + 1) * 128, :],
                    lambda kk, kc=kc: P.wkv_b[kc, :, :], LAT, sl)
        _row_accum_mms(k, P, P.onescol, 0, P.pc_wb[sl], LAT, P.pc_rowg, 0,
                       "onescol", wb, "mrows")
    k.op("v", lambda e: e.tensor_scalar(out=P.pc_rb0[0:1, 0:LAT], in0=P.pc_rowg[0:1, 0:LAT],
                                        scalar1=-1.0, scalar2=None, op0=ALU.mult),
         reads=["mrows"], writes=["mrowb"])
    k.dma("y", lambda kk: P.wkvaug_b, lambda kk: P.pc_rb0[0:1, 0:LAT], reads=["mrowb"])
    _cast_store(k, P, P.awk[:, :], lambda kk: P.wk_b[:, :], 512, 0)
    _cast_store(k, P, P.awv[:, :], lambda kk: P.wv_b[:, :], 512, 1)
    for kc in range(4):
        sl = kc % 2
        wf, wb = f"pc_wf{sl}", f"pc_wb{sl}"
        for j in range(4):
            h = kc * 4 + j
            k.dma("y", lambda kk, j=j, sl=sl: P.pc_wf[sl][j * 32:(j + 1) * 32, 0:D],
                  P.awo[h * 64: h * 64 + 32, :], writes=[wf])
        k.op("p", lambda e, sl=sl: e.tensor_copy(out=P.pc_wb[sl][:, 0:D],
                                                 in_=P.pc_wf[sl][:, 0:D]),
             reads=[wf], writes=[wb])
        k.dma("y", lambda kk, kc=kc: P.womla_b[kc, :, :],
              lambda kk, sl=sl: P.pc_wb[sl][:, 0:D], reads=[wb])


ZW = NCH * 520


def _zv(P, off, n, t=0):
    return bass.AP(P.zin[t], off, [[ZW, 128], [1, n]])


def emit_stage_in(k: K, P: Prog, src=None, write_x=True):
    if src is None:
        src = P.x_in
    k.op("v", lambda e: e.memset(_zv(P, 2080, 8), 0.0), writes=["st_zf"])
    k.dma("y", bass.AP(P.z_hbm.tensor, 0, [[ZCOLS, 128], [128 * ZCOLS, NCH], [1, 1]]),
          lambda kk: bass.AP(P.zin[0], 2080, [[ZW, 128], [1, NCH], [1, 1]]),
          reads=["st_zf"], slow=True)
    for tt in range(32):
        k.dma("y", lambda kk: _zv(P, 0, 1024), src[tt * 128:(tt + 1) * 128, :],
              writes=["stin"], cls="dz")
        k.op("v", lambda e: e.tensor_reduce(out=P.stcol[:, 0:1], in_=_zv(P, 0, 1024),
                                            axis=AX.X, op=ALU.add),
             reads=["stin"], writes=["stS"])
        k.op("p", lambda e: e.tensor_tensor(out=P.x1[:, 0:1024], in0=_zv(P, 0, 1024),
                                            in1=_zv(P, 0, 1024), op=ALU.mult),
             reads=["stin"], writes=["x1"])
        k.op("v", lambda e: e.tensor_reduce(out=P.stcol[:, 1:2], in_=P.x1[:, 0:1024],
                                            axis=AX.X, op=ALU.add),
             reads=["x1"], writes=["stQ"])
        k.op("v", lambda e: e.tensor_scalar(out=P.stcol[:, 2:3], in0=P.stcol[:, 0:1],
                                            scalar1=1.0 / D, scalar2=None, op0=ALU.mult),
             reads=["stS"], writes=["stmu"])
        k.op("v", lambda e: e.tensor_tensor(out=P.stcol[:, 3:4], in0=P.stcol[:, 2:3],
                                            in1=P.stcol[:, 2:3], op=ALU.mult),
             reads=["stmu"], writes=["stmusq"])
        k.op("v", lambda e: e.scalar_tensor_tensor(out=P.stcol[:, 4:5], in0=P.stcol[:, 1:2],
                                                   scalar=1.0 / D, in1=P.stcol[:, 3:4],
                                                   op0=ALU.mult, op1=ALU.subtract),
             reads=["stQ", "stmusq"], writes=["stvar"])
        k.op("s", lambda e: e.activation(out=P.stcol[:, 4:5], in_=P.stcol[:, 4:5],
                                         func=AF.Ln, bias=P.cbias[:, 0:1]),
             reads=["stvar", "cbias"], writes=["stvar"])
        k.op("s", lambda e: e.activation(out=P.stcol[:, 5:6], in_=P.stcol[:, 4:5],
                                         func=AF.Exp, scale=-0.5),
             reads=["stvar"], writes=["strstd"])
        k.op("v", lambda e: e.tensor_scalar(out=_zv(P, 1040, 1024), in0=_zv(P, 0, 1024),
                                            scalar1=P.stcol[:, 2:3], scalar2=P.stcol[:, 5:6],
                                            op0=ALU.subtract, op1=ALU.mult),
             reads=["stin", "stmu", "strstd"], writes=["stin2"])
        passes = [(1040, "stin2", 2080, "st_zf", P.z_hbm)]
        if write_x:
            passes.append((0, "stin", 3120, "st_xf", P.x_hbm))
        for src_off, src_key, dst_off, dst_key, hbm in passes:
            for half in range(2):
                ps = P.ps0 if half == 0 else P.ps1
                pkey = "ps0" if half == 0 else "ps1"
                k.ops("t", [
                    (lambda e, c=c, ps=ps, src_off=src_off: e.transpose(
                        out=ps.ap()[:, (c % 4) * 128:(c % 4) * 128 + 128],
                        in_=_zv(P, src_off + c * 128, 128),
                        identity=P.identf[:]))
                    for c in range(half * 4, half * 4 + 4)],
                    reads=[src_key, "identf"], writes=[pkey])
                k.op("v", lambda e, half=half, ps=ps, dst_off=dst_off: e.tensor_copy(
                    out=_zv(P, dst_off + half * 512, 512), in_=ps.ap()[:]),
                    reads=[pkey], writes=[pkey, dst_key])
            k.dma("y", bass.AP(hbm.tensor, 1 + tt * 128,
                               [[ZCOLS, 128], [128 * ZCOLS, NCH], [1, 128]]),
                  lambda kk, dst_off=dst_off: bass.AP(
                      P.zin[0], dst_off, [[ZW, 128], [128, NCH], [1, 128]]),
                  reads=[dst_key])


def _stage_sections(k, P):
    for name in ("qkv", "wo", "qsrep", "ksrep", "w1", "w1aug", "w2", "w2aug"):
        off = P.SEC[name]
        size = P.SECSZ[name]
        rows = size // 2048
        cls = {"qkv": "dga", "wo": "dga", "qsrep": "dga", "ksrep": "dga",
               "w1": "dgb", "w1aug": "dgb", "w2": "dgc", "w2aug": "dgc"}[name]
        k.dma("y", bass.AP(P.stage.tensor, off, [[2048, rows], [1, 2048]]),
              lambda kk, off=off, rows=rows: bass.AP(
                  P.wslab.tensor, kk.i * P.SLAB_E + off,
                  [[2048, rows], [1, 2048]]),
              writes=[f"stg_{name}"], cls=cls)


def _layer_setup(k, P):
    k.dma("y", lambda kk: P.qsrep[:],
          bass.AP(P.stage.tensor, P.SEC["qsrep"], [[512, 128], [1, 512]]),
          reads=["stg_qsrep"], writes=["qsrep"], cls="dw")
    k.dma("y", lambda kk: P.ksrep[:],
          bass.AP(P.stage.tensor, P.SEC["ksrep"], [[512, 128], [1, 512]]),
          reads=["stg_ksrep"], writes=["ksrep"], cls="dw")
    k.dma("y", lambda kk: P.w1augt[:],
          bass.AP(P.stage.tensor, P.SEC["w1aug"], [[W1C, 2], [1, W1C]]),
          reads=["stg_w1aug"], writes=["w1augt"], cls="dw")
    k.dma("y", lambda kk: P.b2row[:],
          bass.AP(P.stage.tensor, P.SEC["w2aug"], [[D, 1], [1, D]]),
          reads=["stg_w2aug"], writes=["b2row"], cls="dw")


SC520 = 520


def _zin_ap3(zt, c0=0, n=513, coloff=0):
    return bass.AP(zt, c0 * SC520 + coloff, [[NCH * SC520, 128], [SC520, NCH - c0], [1, n]])


def emit_f_call(k: K, P: Prog, call_idx):
    with k.loop(P.n_layers, f"fcall{call_idx}") as L:
        _stage_sections(k, P)
        _layer_setup(k, P)
        _z_load(k, P, 0)
        if TAP_QKV[0]:
            k.dma("y", bass.AP(P.tap_hbm.tensor, 16384, [[57344, 128], [1, 8320]]),
                  lambda kk: P.zin[0][:].bitcast(BF16), reads=["zin0"])
        for s in range(NS):
            emit_stripe_attn(k, P, s)
            emit_attn_units(k, P, s)
            emit_wo_resid(k, P, s)
            emit_stripe_ffn(k, P, s)


def _z_load(k, P, s):
    zt = P.zin[s % 2]
    zkey = f"zin{s % 2}"
    k.dma("y", lambda kk, zt=zt: _zin_ap3(zt),
          bass.AP(P.z_hbm.tensor, s * 512,
                  [[ZCOLS, 128], [128 * ZCOLS, NCH], [1, 513]]),
          reads=[f"zhw{s}"], writes=[zkey], cls="dz")


def emit_stripe_attn(k: K, P: Prog, s):
    zt = P.zin[s % 2]
    zkey = f"zin{s % 2}"
    k.op("v", lambda e: e.tensor_copy(out=bass.AP(P.x1, 0, [[NCH * SC520, 128], [1, NCH * SC520]]),
                                      in_=bass.AP(zt, 0, [[NCH * SC520, 128], [1, NCH * SC520]])),
         reads=[zkey], writes=["x1"])
    tabs = (P.c1, P.c2s, P.c3, P.c4s)
    for ts in range(NTS):
        t0 = s * 512 + ts * 128
        for ci, tab in enumerate(tabs):
            k.dma("y", lambda kk, ts=ts, ci=ci: bass.AP(
                P.crep, ts * 2048 + ci * 512, [[NTS * 2048, 128], [1, 512]]),
                tab[t0:t0 + 128, :, :].rearrange("t h d -> t (h d)"),
                writes=[f"crep{ts}_{ci}"], cls="dw")
    for ob in range(6):
        wsl = ob % 2
        wkey = f"wbuf{wsl}"
        k.dma("y", lambda kk, wsl=wsl: P.wbuf[wsl].ap().rearrange("p (a b) -> p a b", a=NCH),
              bass.AP(P.stage.tensor, P.SEC["qkv"] + ob * 512,
                      [[3 * D, 128], [128 * 3 * D, NCH], [1, 512]]),
              reads=["stg_qkv"], writes=[wkey], cls="dw")
        for ts in range(NTS):
            pkey = "ps0" if (ob * NTS + ts) % 2 == 0 else "ps1"
            ps = P.ps0 if pkey == "ps0" else P.ps1
            def mk_mm(kc, ps=ps, ts=ts, wsl=wsl):
                def f(e):
                    shift = 1 if kc < 4 else 0
                    return e.matmul(
                        out=ps.ap()[:, 0:512],
                        lhsT=bass.AP(P.x1, kc * SC520 + shift + ts * 128,
                                     [[NCH * SC520, 128], [1, 128]]),
                        rhs=P.wbuf[wsl][:, kc * 512:(kc + 1) * 512],
                        start=(kc == 0), stop=(kc == 7))
                return f
            k.ops("t", [mk_mm(kc) for kc in range(8)],
                  reads=["x1", wkey], writes=[pkey])
            _qkv_post(k, P, s, ts, ob, ps, pkey)


def _qkv_post(k, P, s, ts, ob, ps, pkey):
    rv = (4 * s + ts) % 6
    sl = (ob * NTS + ts) % 2
    SQ, UT, T1, T2, URF, UROT, NRM, RK = (P.sq, P.u_t, P.t1, P.t2,
                                          P.urf[sl], P.urot[sl], P.nrm[sl], P.rk[sl])
    kq = lambda n: f"{n}{sl}" if n not in ("u", "t1", "t2a", "t2b", "sq") else n
    if ob >= 4:
        k.op("v", lambda e: e.tensor_copy(
            out=bass.AP(P.vring, rv * 1024 + (ob - 4) * 512, [[6 * 1024, 128], [1, 512]]),
            in_=ps.ap()[:]),
            reads=[pkey], writes=[pkey] + [f"vr_{rv}_{h}" for h in range((ob - 4) * 8, (ob - 3) * 8)])
        if TAP_QKV[0] and s == 0 and ts == 0 and ob == 4:
            tb3 = NCH * 512 + NCH * 768 + 6 * 1024 + NTS * 2048 + 4608
            k.dma("y", bass.AP(P.tap_hbm.tensor, tb3, [[57344, 128], [1, NCH * SC520]]),
                  lambda kk: bass.AP(P.x1, 0, [[NCH * SC520, 128], [1, NCH * SC520]]),
                  reads=["x1"])
            k.dma("y", bass.AP(P.tap_hbm.tensor, tb3 + NCH * SC520,
                               [[57344, 128], [1, 8 * 512]]),
                  lambda kk: P.wbuf[0][:], reads=["wbuf0"])
            k.dma("y", bass.AP(P.tap_hbm.tensor, tb3 + NCH * SC520 + 4096,
                               [[57344, 128], [1, 512]]),
                  lambda kk: bass.AP(P.vring, rv * 1024, [[6 * 1024, 128], [1, 512]]),
                  reads=[f"vr_{rv}_{h}" for h in range(8)])
        return
    is_q = ob < 2
    rep = P.qsrep if is_q else P.ksrep
    repkey = "qsrep" if is_q else "ksrep"
    k.op("s", lambda e: e.activation(out=SQ[:], in_=ps.ap()[:], func=AF.Square),
         reads=[pkey], writes=[kq("sq")])
    k.op("v", lambda e: e.tensor_reduce(
        out=NRM[:], in_=SQ.ap().rearrange("p (h d) -> p h d", d=64),
        axis=AX.X, op=ALU.add),
        reads=[kq("sq")], writes=[kq("nrm")])
    k.op("v", lambda e: e.tensor_tensor(out=UT[:], in0=ps.ap()[:], in1=rep[:],
                                        op=ALU.mult),
         reads=[pkey, repkey], writes=[pkey, kq("u")])
    ci_cos = 0 if is_q else 2
    ci_sin = 1 if is_q else 3
    ck = f"crep{ts}_{ci_cos}"
    sk = f"crep{ts}_{ci_sin}"
    cosap = bass.AP(P.crep, ts * 2048 + ci_cos * 512, [[NTS * 2048, 128], [1, 512]])
    sin3 = lambda h0: bass.AP(P.crep, ts * 2048 + ci_sin * 512 + h0,
                              [[NTS * 2048, 128], [64, 8], [1, 32]])
    u3 = lambda h0: bass.AP(UT, h0, [[512, 128], [64, 8], [1, 32]])
    t23 = lambda h0: bass.AP(T2, h0, [[512, 128], [64, 8], [1, 32]])
    k.op("v", lambda e: e.tensor_tensor(out=T1[:], in0=UT[:], in1=cosap, op=ALU.mult),
         reads=[kq("u"), ck], writes=[kq("t1")])
    k.op("v", lambda e: e.tensor_tensor(out=t23(0), in0=u3(32), in1=sin3(0), op=ALU.mult),
         reads=[kq("u"), sk], writes=[kq("t2a")])
    k.op("v", lambda e: e.tensor_tensor(out=t23(32), in0=u3(0), in1=sin3(32), op=ALU.mult),
         reads=[kq("u"), sk], writes=[kq("t2b")])
    k.op("p", lambda e: e.tensor_tensor(out=URF[:], in0=T1[:], in1=T2[:], op=ALU.add),
         reads=[kq("t1"), kq("t2a"), kq("t2b")], writes=[kq("urf")])
    k.op("s", lambda e: e.activation(out=NRM[:], in_=NRM[:], func=AF.Ln),
         reads=[kq("nrm")], writes=[kq("nrm")])
    k.op("s", lambda e: e.activation(out=NRM[:], in_=NRM[:], func=AF.Exp, scale=0.5),
         reads=[kq("nrm")], writes=[kq("nrm")])
    k.op("v", lambda e: e.tensor_scalar(out=NRM[:], in0=NRM[:], scalar1=1e-12,
                                        scalar2=None, op0=ALU.max),
         reads=[kq("nrm")], writes=[kq("nrm")])
    k.op("v", lambda e: e.reciprocal(out=RK[:], in_=NRM[:]),
         reads=[kq("nrm")], writes=[kq("rk")])
    for h in range(8):
        k.op("p", lambda e, h=h: e.tensor_scalar(
            out=UROT[:, h * 64:(h + 1) * 64], in0=URF[:, h * 64:(h + 1) * 64],
            scalar1=RK[:, h:h + 1], scalar2=None, op0=ALU.mult),
            reads=[kq("urf"), kq("rk")], writes=[kq("urot")])
    k.ops("t", [
        (lambda e, i=i: e.transpose(out=P.psb.ap()[:, sl * 512 + i * 128: sl * 512 + (i + 1) * 128],
                                    in_=UROT[:, i * 128:(i + 1) * 128],
                                    identity=P.ident[:]))
        for i in range(4)],
        reads=[kq("urot"), "ident"], writes=[kq("psb")])
    if is_q:
        dst_ap = bass.AP(P.qhat, ob * 4 * 512 + ts * 128,
                         [[NCH * 512, 128], [512, 4], [1, 128]])
        wkeys = [f"qh_{c}_{ts}" for c in range(ob * 4, ob * 4 + 4)]
    else:
        dst_ap = bass.AP(P.khat, (ob - 2) * 4 * 768 + rvcol(s, ts),
                         [[NCH * 768, 128], [768, 4], [1, 128]])
        wkeys = [f"kh_{c}_{rvidx(s, ts)}" for c in range((ob - 2) * 4, (ob - 2) * 4 + 4)]
    k.op("v", lambda e: e.tensor_copy(
        out=dst_ap, in_=P.psb.ap()[:, sl * 512: sl * 512 + 512].rearrange(
            "p (a b) -> p a b", a=4)),
        reads=[kq("psb")], writes=[kq("psb")] + wkeys)


def rvidx(s, ts):
    return (4 * s + ts) % 6


def rvcol(s, ts):
    return ((4 * s + ts) % 6) * 128


def emit_attn_units(k: K, P: Prog, s):
    pend = [None]

    def stage2(u):
        if u is None:
            return
        hp, qt, nblk, par = u
        pk = f"u{par}"
        k.op("v", lambda e: e.reciprocal(out=P.rsum[par][:], in_=P.sums[par][:]),
             reads=[pk + "sums"], writes=[pk + "rsum"])
        for j in range(2):
            k.op("v", lambda e, j=j: e.tensor_scalar(
                out=P.diag2[par][:, j * 128:(j + 1) * 128], in0=P.ident[:],
                scalar1=P.rsum[par][:, j:j + 1], scalar2=None, op0=ALU.mult),
                reads=[pk + "rsum", "ident"], writes=[pk + f"diag{j}"])
        def mk_pt(j, bi, nblk=nblk, par=par):
            def f(e):
                return e.matmul(
                    out=P.ps45.ap()[:, j * 512 + bi * 128: j * 512 + (bi + 1) * 128],
                    lhsT=P.P_t[par][:, j * 384 + bi * 128: j * 384 + (bi + 1) * 128],
                    rhs=P.diag2[par][:, j * 128:(j + 1) * 128],
                    start=(bi == 0), stop=(bi == nblk - 1))
            return f
        for j in range(2):
            k.ops("t", [mk_pt(j, bi) for bi in range(nblk)],
                  reads=[pk + "P", pk + f"diag{j}"], writes=["ps45"])
        k.op("v", lambda e: e.tensor_copy(
            out=bass.AP(P.Pt_sb[par], 0, [[768, 128], [384, 2], [1, nblk * 128]]),
            in_=bass.AP(P.ps45, 0, [[1024, 128], [512, 2], [1, nblk * 128]])),
            reads=["ps45"], writes=["ps45", pk + "Pt"])
        gq = 4 * s + qt
        blocks = list(range(max(0, gq - 2), gq + 1))
        vkeys = []
        mms = []
        first = True
        for j in range(2):
            for bi, gb in enumerate(blocks):
                rv = gb % 6
                h = hp * 2 + j
                vkeys.append(f"vr_{rv}_{h}")
                def f(e, j=j, bi=bi, rv=rv, h=h, first=first, par=par,
                      last=(j == 1 and bi == nblk - 1)):
                    return e.matmul(
                        out=P.ps6.ap()[0:64, j * 128:(j + 1) * 128],
                        lhsT=P.vring[:, rv * 1024 + h * 64: rv * 1024 + h * 64 + 64],
                        rhs=P.Pt_sb[par][:, j * 384 + bi * 128: j * 384 + (bi + 1) * 128],
                        start=first, stop=last)
                mms.append(f)
                first = False
        k.ops("t", mms, reads=[pk + "Pt"] + vkeys, writes=["ps6"])
        for j in range(2):
            k.op("v", lambda e, j=j, hp=hp, qt=qt: e.tensor_copy(
                out=P.ao[j * 64:(j + 1) * 64, hp * 512 + qt * 128: hp * 512 + (qt + 1) * 128],
                in_=P.ps6.ap()[0:64, j * 128:(j + 1) * 128]),
                reads=["ps6"], writes=["ps6", f"ao_{hp}_{qt}_{j}"])

    for hp in range(8):
        for qt in range(NTS):
            gq = 4 * s + qt
            par = (hp * NTS + qt) % 2
            pk = f"u{par}"
            blocks = []
            if gq >= 2:
                blocks.append((gq - 2, "maskU"))
            if gq >= 1:
                blocks.append((gq - 1, None))
            blocks.append((gq, "maskL"))
            nblk = len(blocks)
            qkeys = [f"qh_{hp}_{qt}"]
            kkeys = [f"kh_{hp}_{gb % 6}" for gb, _ in blocks]
            if par == 0:
                pst = [(P.ps23, 0), (P.ps23, 512)]
                psk = ["ps23", "ps23"]
            else:
                pst = [(P.ps0, 0), (P.ps1, 0)]
                psk = ["ps0", "ps1"]
            def mk_sim(j, bi, gb, nblk=nblk, hp=hp, qt=qt):
                ps, c0 = pst[j]
                def f(e):
                    return e.matmul(
                        out=ps.ap()[:, c0 + bi * 128: c0 + (bi + 1) * 128],
                        lhsT=P.qhat[j * 64:(j + 1) * 64,
                                     hp * 512 + qt * 128: hp * 512 + (qt + 1) * 128],
                        rhs=P.khat[j * 64:(j + 1) * 64,
                                   hp * 768 + (gb % 6) * 128: hp * 768 + (gb % 6) * 128 + 128],
                        start=(bi == 0), stop=(bi == nblk - 1),
                        tile_position=(j * 64, 0))
                return f
            for j in range(2):
                k.ops("t", [mk_sim(j, bi, gb) for bi, (gb, _) in enumerate(blocks)],
                      reads=qkeys + kkeys, writes=[psk[j]])
            for bi, (gb, mk) in enumerate(blocks):
                if mk is None:
                    continue
                mt = P.maskU2 if mk == "maskU" else P.maskL2
                if par == 0:
                    k.op("v", lambda e, bi=bi, mt=mt: e.tensor_tensor(
                        out=bass.AP(P.ps23, bi * 128, [[1024, 128], [512, 2], [1, 128]]),
                        in0=bass.AP(P.ps23, bi * 128, [[1024, 128], [512, 2], [1, 128]]),
                        in1=bass.AP(mt, 0, [[256, 128], [128, 2], [1, 128]]),
                        op=ALU.add),
                        reads=["ps23", mk], writes=["ps23"])
                else:
                    for j in range(2):
                        ps, c0 = pst[j]
                        k.op("v", lambda e, bi=bi, mt=mt, ps=ps, c0=c0, j=j: e.tensor_tensor(
                            out=ps.ap()[:, c0 + bi * 128: c0 + (bi + 1) * 128],
                            in0=ps.ap()[:, c0 + bi * 128: c0 + (bi + 1) * 128],
                            in1=mt[:, j * 128:(j + 1) * 128],
                            op=ALU.add),
                            reads=[psk[j], mk], writes=[psk[j]])
            if par == 0:
                k.op("v", lambda e, nblk=nblk: e.tensor_reduce(
                    out=P.negm[par][:],
                    in_=bass.AP(P.ps23, 0, [[1024, 128], [512, 2], [1, nblk * 128]]),
                    axis=AX.X, op=ALU.max, negate=True),
                    reads=["ps23"], writes=[pk + "negm"])
            else:
                for j in range(2):
                    ps, c0 = pst[j]
                    k.op("v", lambda e, nblk=nblk, ps=ps, c0=c0, j=j: e.tensor_reduce(
                        out=P.negm[par][:, j:j + 1],
                        in_=ps.ap()[:, c0: c0 + nblk * 128],
                        axis=AX.X, op=ALU.max, negate=True),
                        reads=[psk[j]], writes=[pk + "negm"])
            k.op("v", lambda e: e.tensor_scalar(
                out=P.nb[par][:], in0=P.negm[par][:], scalar1=8.0, scalar2=None,
                op0=ALU.mult),
                reads=[pk + "negm"], writes=[pk + "nb"])
            for j in range(2):
                ps, c0 = pst[j]
                k.op("s", lambda e, j=j, nblk=nblk, ps=ps, c0=c0: e.activation(
                    out=P.P_t[par][:, j * 384: j * 384 + nblk * 128],
                    in_=ps.ap()[:, c0: c0 + nblk * 128],
                    func=AF.Exp,
                    scale=8.0,
                    bias=P.nb[par][:, j: j + 1],
                    accum_out=P.sums[par][:, j:j + 1]),
                    reads=[psk[j], pk + "nb"],
                    writes=[pk + "P", pk + "sums"])
            if TAP_QKV[0] and s == 0 and hp == 0 and qt == 0:
                TB = 45056 + 4096
                k.dma("y", bass.AP(P.tap_hbm.tensor, TB + 2048, [[57344, 128], [1, 768]]),
                      lambda kk: P.P_t[par][:], reads=[pk + "P"])
                k.dma("y", bass.AP(P.tap_hbm.tensor, TB + 2816, [[57344, 128], [1, 4]]),
                      lambda kk: P.negm[par][:].bitcast(BF16), reads=[pk + "negm"])
                k.dma("y", bass.AP(P.tap_hbm.tensor, TB + 2820, [[57344, 128], [1, 4]]),
                      lambda kk: P.nb[par][:].bitcast(BF16), reads=[pk + "nb"])
                k.dma("y", bass.AP(P.tap_hbm.tensor, TB + 2824, [[57344, 128], [1, 4]]),
                      lambda kk: P.sums[par][:].bitcast(BF16), reads=[pk + "sums"])
            stage2(pend[0])
            pend[0] = (hp, qt, nblk, par)
    stage2(pend[0])
    pend[0] = None


def emit_wo_resid(k: K, P: Prog, s):
    zt = P.zin[s % 2]
    zkey = f"zin{s % 2}"
    aokeys = [f"ao_{hp}_{qt}_{j}" for hp in range(8) for qt in range(NTS) for j in range(2)]
    for oc in range(NCH):
        wsl = oc % 2
        wkey = f"wbuf{wsl}"
        k.dma("y", lambda kk, wsl=wsl: P.wbuf[wsl].ap()[:, 0:NCH * 128].rearrange(
            "p (a b) -> p a b", a=NCH),
              bass.AP(P.stage.tensor, P.SEC["wo"] + oc * 128,
                      [[D, 128], [128 * D, NCH], [1, 128]]),
              reads=["stg_wo"], writes=[wkey], cls="dw")
        pkey = "ps0" if oc % 2 == 0 else "ps1"
        ps = P.ps0 if oc % 2 == 0 else P.ps1
        def mk(kc, ps=ps, wsl=wsl):
            def f(e):
                return e.matmul(
                    out=ps.ap()[:, 0:512],
                    lhsT=P.wbuf[wsl][:, kc * 128:(kc + 1) * 128],
                    rhs=P.ao[:, kc * 512:(kc + 1) * 512],
                    start=(kc == 0), stop=(kc == 7))
            return f
        k.ops("t", [mk(kc) for kc in range(8)],
              reads=[wkey] + aokeys, writes=[pkey])
        k.op("v", lambda e, oc=oc, ps=ps, zt=zt: e.tensor_tensor(
            out=bass.AP(zt, oc * SC520 + 1, [[NCH * SC520, 128], [1, 512]]),
            in0=bass.AP(zt, oc * SC520 + 1, [[NCH * SC520, 128], [1, 512]]),
            in1=ps.ap()[:], op=ALU.add),
            reads=[pkey, zkey], writes=[pkey, zkey])


TAP_ATTN = [False]
TAP_QKV = [False]
TAP_MLA = [False]
TAPQT = [1]


def emit_stripe_ffn(k: K, P: Prog, s):
    zt = P.zin[s % 2]
    zkey = f"zin{s % 2}"
    if TAP_QKV[0] and s == 0:
        qk_all = [f"qh_{c}_{t}" for c in range(8) for t in range(4)] + \
                 [f"kh_{c}_{r}" for c in range(8) for r in range(4)] + \
                 [f"vr_{r}_{h}" for r in range(4) for h in range(16)]
        k.dma("y", bass.AP(P.tap_hbm.tensor, NCH * 512 + NCH * 768 + 6 * 1024,
                           [[57344, 128], [1, NTS * 2048]]),
              lambda kk: P.crep[:], reads=qk_all)
        k.dma("y", bass.AP(P.tap_hbm.tensor, 45056, [[57344, 128], [1, NCH * 512]]),
              lambda kk: P.ao[:],
              reads=[f"ao_{hp}_{qt}_{j}" for hp in range(8) for qt in range(4) for j in range(2)])
        tb2 = NCH * 512 + NCH * 768 + 6 * 1024 + NTS * 2048 + 3584
        k.dma("y", bass.AP(P.tap_hbm.tensor, tb2, [[57344, 128], [1, 768]]),
              lambda kk: P.P_t[1][:], reads=["u1P"] + qk_all)
        k.dma("y", bass.AP(P.tap_hbm.tensor, tb2 + 768, [[57344, 128], [1, 4]]),
              lambda kk: P.sums[1][:].bitcast(BF16), reads=["u1sums"] + qk_all)
        k.dma("y", bass.AP(P.tap_hbm.tensor, 0, [[57344, 128], [1, NCH * 512]]),
              lambda kk: P.qhat[:], reads=qk_all)
        k.dma("y", bass.AP(P.tap_hbm.tensor, NCH * 512, [[57344, 128], [1, NCH * 768]]),
              lambda kk: P.khat[:], reads=qk_all)
        k.dma("y", bass.AP(P.tap_hbm.tensor, NCH * 512 + NCH * 768,
                           [[57344, 128], [1, 6 * 1024]]),
              lambda kk: P.vring[:], reads=qk_all)
    if TAP_ATTN[0]:
        k.dma("y", bass.AP(P.y_hbm.tensor, s * 512 + 1,
                           [[ZCOLS, 128], [128 * ZCOLS, NCH], [1, 512]]),
              lambda kk, zt=zt: bass.AP(zt, 1, [[NCH * SC520, 128], [SC520, NCH], [1, 512]]),
              reads=[zkey])
    if s > 0:
        k.op("v", lambda e, zt=zt: e.tensor_copy(
            out=bass.AP(zt, 0, [[NCH * SC520, 128], [SC520, NCH], [1, 1]]),
            in_=P.zcarry.ap().rearrange("p (c o) -> p c o", o=1)),
            reads=["zcarry", zkey], writes=[zkey + "c0"])
    else:
        k.op("v", lambda e, zt=zt: e.memset(
            bass.AP(zt, 0, [[NCH * SC520, 128], [SC520, NCH], [1, 1]]), 0.0),
            reads=[zkey], writes=[zkey + "c0"])
    k.op("v", lambda e, zt=zt: e.tensor_copy(
        out=P.zcarry.ap().rearrange("p (c o) -> p c o", o=1),
        in_=bass.AP(zt, 512, [[NCH * SC520, 128], [SC520, NCH], [1, 1]])),
        reads=[zkey, zkey + "c0"], writes=["zcarry"])
    k.op("p", lambda e, zt=zt: e.tensor_tensor(
        out=bass.AP(P.x1, 0, [[NCH * SC520, 128], [1, NCH * SC520]]),
        in0=bass.AP(zt, 0, [[NCH * SC520, 128], [1, NCH * SC520]]),
        in1=bass.AP(zt, 0, [[NCH * SC520, 128], [1, NCH * SC520]]), op=ALU.mult),
        reads=[zkey, zkey + "c0"], writes=["x1"])
    def mk_s(kc, zt=zt):
        def f(e):
            sh = 1 if kc < 4 else 0
            return e.matmul(out=P.ps0.ap()[0:1, 0:512], lhsT=P.onescol_f[:, 0:1],
                            rhs=bass.AP(zt, kc * SC520 + sh, [[NCH * SC520, 128], [1, 512]]),
                            start=(kc == 0), stop=(kc == 7))
        return f
    k.ops("t", [mk_s(kc) for kc in range(8)],
          reads=[zkey, zkey + "c0", "onescol_f"], writes=["ps0"])
    def mk_q(kc):
        def f(e):
            sh = 1 if kc < 4 else 0
            return e.matmul(out=P.ps1.ap()[0:1, 0:512], lhsT=P.onescol[:, 0:1],
                            rhs=bass.AP(P.x1, kc * SC520 + sh, [[NCH * SC520, 128], [1, 512]]),
                            start=(kc == 0), stop=(kc == 7))
        return f
    k.ops("t", [mk_q(kc) for kc in range(8)],
          reads=["x1", "onescol"], writes=["ps1"])
    st = P.stats
    k.op("v", lambda e: e.tensor_scalar(out=st[0:1, 0:512], in0=P.ps0.ap()[0:1, 0:512],
                                        scalar1=1.0 / D, scalar2=None, op0=ALU.mult),
         reads=["ps0"], writes=["ps0", "st_mu"])
    k.op("v", lambda e: e.tensor_tensor(out=st[0:1, 512:1024], in0=st[0:1, 0:512],
                                        in1=st[0:1, 0:512], op=ALU.mult),
         reads=["st_mu"], writes=["st_musq"])
    k.op("v", lambda e: e.scalar_tensor_tensor(out=st[0:1, 1024:1536],
                                               in0=P.ps1.ap()[0:1, 0:512],
                                               scalar=1.0 / D, in1=st[0:1, 512:1024],
                                               op0=ALU.mult, op1=ALU.subtract),
         reads=["ps1", "st_musq"], writes=["ps1", "st_var"])
    k.op("s", lambda e: e.activation(out=st[0:1, 1024:1536], in_=st[0:1, 1024:1536],
                                     func=AF.Ln, bias=P.cbias[0:1, 0:1]),
         reads=["st_var", "cbias"], writes=["st_var"])
    k.op("s", lambda e: e.activation(out=st[0:1, 1536:2048], in_=st[0:1, 1024:1536],
                                     func=AF.Exp, scale=-0.5),
         reads=["st_var"], writes=["st_rstd"])
    k.op("v", lambda e: e.tensor_tensor(out=P.mur2[0:1, :], in0=st[0:1, 0:512],
                                        in1=st[0:1, 1536:2048], op=ALU.mult),
         reads=["st_mu", "st_rstd"], writes=["mur2r0"])
    k.op("v", lambda e: e.tensor_copy(out=P.xnaug[:], in_=P.mur2[:]),
         reads=["mur2r0", "mur2r1"], writes=["xnaug"])
    k.op("v", lambda e: e.tensor_copy(out=P.rstdrow[0:1, :], in_=st[0:1, 1536:2048]),
         reads=["st_rstd"], writes=["rstdrow"])
    k.op("t", lambda e: e.matmul(out=P.ps0.ap()[:, 0:512], lhsT=P.ones1[:, :],
                                 rhs=P.rstdrow[0:1, :], start=True, stop=True),
         reads=["rstdrow", "ones1"], writes=["ps0"])
    k.op("v", lambda e: e.tensor_copy(out=P.rstdb[:], in_=P.ps0.ap()[:, 0:512]),
         reads=["ps0"], writes=["ps0", "rstdb"])
    for kc in range(NCH):
        sh = 1 if kc < 4 else 0
        k.op("p", lambda e, kc=kc, sh=sh, zt=zt: e.tensor_tensor(
            out=P.xn[:, kc * 512:(kc + 1) * 512],
            in0=bass.AP(zt, kc * SC520 + sh, [[NCH * SC520, 128], [1, 512]]),
            in1=P.rstdb[:], op=ALU.mult),
            reads=[zkey, zkey + "c0", "rstdb"], writes=[f"xn{kc}"])
    xnkeys = [f"xn{kc}" for kc in range(NCH)]
    for ob in range(11):
        wsl = ob % 2
        wkey = f"wbuf{wsl}"
        k.dma("y", lambda kk, wsl=wsl: P.wbuf[wsl].ap().rearrange("p (a b) -> p a b", a=NCH),
              bass.AP(P.stage.tensor, P.SEC["w1"] + ob * 512,
                      [[W1C, 128], [128 * W1C, NCH], [1, 512]]),
              reads=["stg_w1"], writes=[wkey], cls="dw")
        for pr in range(2):
            for half, pname, ps in ((0, "ps0", P.ps0), (1, "ps1", P.ps1)):
                m = pr * 2 + half
                def mk(kc, m=m, ps=ps, wsl=wsl):
                    def f(e):
                        return e.matmul(
                            out=ps.ap()[:, 0:512],
                            lhsT=P.wbuf[wsl][:, kc * 512 + m * 128: kc * 512 + (m + 1) * 128],
                            rhs=P.xn[:, kc * 512:(kc + 1) * 512],
                            start=(kc == 0), stop=False)
                    return f
                def mk_aug(m=m, ps=ps, ob=ob):
                    def f(e):
                        return e.matmul(
                            out=ps.ap()[:, 0:512],
                            lhsT=P.w1augt[0:2, ob * 512 + m * 128: ob * 512 + (m + 1) * 128],
                            rhs=P.xnaug[0:2, :],
                            start=False, stop=True)
                    return f
                k.ops("t", [mk(kc) for kc in range(8)] + [mk_aug()],
                      reads=xnkeys + [wkey, "xnaug", "w1augt"], writes=[pname])
            pair = ob * 2 + pr
            k.op("s", lambda e: e.activation(out=P.gl[:], in_=P.ps1.ap()[:], func=AF.Gelu),
                 reads=["ps1"], writes=["ps1", "gl"])
            k.op("v", lambda e, pair=pair: e.scalar_tensor_tensor(
                out=P.h_t[:, pair * 512:(pair + 1) * 512], in0=P.ps0.ap()[:],
                scalar=1.0, in1=P.gl[:], op0=ALU.mult, op1=ALU.mult),
                reads=["ps0", "gl"], writes=["ps0", f"h{pair}"])
    hkeys = [f"h{p}" for p in range(NIC)]
    for oc in range(NCH):
        wsl = oc % 2
        wkey = f"w2buf{wsl}"
        k.dma("y", lambda kk, wsl=wsl, oc=oc: P.w2buf[wsl].ap().rearrange(
            "p (a b) -> p a b", a=NIC),
              bass.AP(P.stage.tensor, P.SEC["w2"] + oc * 128,
                      [[D, 128], [128 * D, NIC], [1, 128]]),
              reads=["stg_w2"], writes=[wkey], cls="dw")
        pname = "ps0" if oc % 2 == 0 else "ps1"
        ps = P.ps0 if oc % 2 == 0 else P.ps1
        def mk2(kc, ps=ps, wsl=wsl):
            def f(e):
                return e.matmul(
                    out=ps.ap()[:, 0:512],
                    lhsT=P.w2buf[wsl][:, kc * 128:(kc + 1) * 128],
                    rhs=P.h_t[:, kc * 512:(kc + 1) * 512],
                    start=(kc == 0), stop=False)
            return f
        def mk2_aug(oc=oc, ps=ps):
            def f(e):
                return e.matmul(out=ps.ap()[:, 0:512],
                                lhsT=P.b2row[0:1, oc * 128:(oc + 1) * 128],
                                rhs=P.haug[0:1, :], start=False, stop=True)
            return f
        k.ops("t", [mk2(kc) for kc in range(NIC)] + [mk2_aug()],
              reads=hkeys + [wkey, "b2row", "haug"], writes=[pname])
        k.op("v", lambda e, oc=oc, ps=ps, zt=zt: e.tensor_tensor(
            out=bass.AP(zt, oc * SC520 + 1, [[NCH * SC520, 128], [1, 512]]),
            in0=bass.AP(zt, oc * SC520 + 1, [[NCH * SC520, 128], [1, 512]]),
            in1=ps.ap()[:], op=ALU.add),
            reads=[pname, zkey], writes=[pname, zkey])
    if s + 1 < NS:
        _z_load(k, P, s + 1)
    k.dma("y", bass.AP(P.z_hbm.tensor, s * 512 + 1,
                       [[ZCOLS, 128], [128 * ZCOLS, NCH], [1, 512]]),
          lambda kk, zt=zt: bass.AP(zt, 1, [[NCH * SC520, 128], [SC520, NCH], [1, 512]]),
          reads=[zkey], writes=[f"zhw{s + 1}"])


def emit_stage_y1(k, P):
    st = P.stats
    for s in range(NS):
        k.dma("y", lambda kk: bass.AP(P.zin[0], 0, [[ZW, 128], [512, NCH], [1, 512]]),
              bass.AP(P.z_hbm.tensor, 1 + s * 512,
                      [[ZCOLS, 128], [128 * ZCOLS, NCH], [1, 512]]),
              writes=["stin"], cls="dz")
        k.dma("y", lambda kk: bass.AP(P.zin[1], 0, [[ZW, 128], [512, NCH], [1, 512]]),
              bass.AP(P.x_hbm.tensor, 1 + s * 512,
                      [[ZCOLS, 128], [128 * ZCOLS, NCH], [1, 512]]),
              writes=["stin2"], cls="dw")
        k.op("v", lambda e: e.scalar_tensor_tensor(
            out=bass.AP(P.zin[1], 0, [[ZW, 128], [1, NCH * 512]]),
            in0=bass.AP(P.zin[0], 0, [[ZW, 128], [1, NCH * 512]]), scalar=0.5,
            in1=bass.AP(P.zin[1], 0, [[ZW, 128], [1, NCH * 512]]),
            op0=ALU.mult, op1=ALU.add),
            reads=["stin", "stin2"], writes=["y1"])
        k.dma("y", bass.AP(P.y_hbm.tensor, 1 + s * 512,
                           [[ZCOLS, 128], [128 * ZCOLS, NCH], [1, 512]]),
              lambda kk: bass.AP(P.zin[1], 0, [[ZW, 128], [512, NCH], [1, 512]]),
              reads=["y1"])
        def mk_s(kc):
            def f(e):
                return e.matmul(out=P.ps0.ap()[0:1, 0:512], lhsT=P.onescol_f[:, 0:1],
                                rhs=bass.AP(P.zin[1], kc * 512, [[ZW, 128], [1, 512]]),
                                start=(kc == 0), stop=(kc == 7))
            return f
        k.ops("t", [mk_s(kc) for kc in range(NCH)],
              reads=["y1", "onescol_f"], writes=["ps0"])
        k.op("p", lambda e: e.tensor_tensor(
            out=P.x1[:, 0:NCH * 512],
            in0=bass.AP(P.zin[1], 0, [[ZW, 128], [1, NCH * 512]]),
            in1=bass.AP(P.zin[1], 0, [[ZW, 128], [1, NCH * 512]]), op=ALU.mult),
            reads=["y1"], writes=["x1"])
        def mk_q(kc):
            def f(e):
                return e.matmul(out=P.ps1.ap()[0:1, 0:512], lhsT=P.onescol[:, 0:1],
                                rhs=P.x1[:, kc * 512:(kc + 1) * 512],
                                start=(kc == 0), stop=(kc == 7))
            return f
        k.ops("t", [mk_q(kc) for kc in range(NCH)],
              reads=["x1", "onescol"], writes=["ps1"])
        k.op("v", lambda e: e.tensor_scalar(out=st[0:1, 0:512], in0=P.ps0.ap()[0:1, 0:512],
                                            scalar1=1.0 / D, scalar2=None, op0=ALU.mult),
             reads=["ps0"], writes=["ps0", "st_mu"])
        k.op("v", lambda e: e.tensor_tensor(out=st[0:1, 512:1024], in0=st[0:1, 0:512],
                                            in1=st[0:1, 0:512], op=ALU.mult),
             reads=["st_mu"], writes=["st_musq"])
        k.op("v", lambda e: e.scalar_tensor_tensor(out=st[0:1, 1024:1536],
                                                   in0=P.ps1.ap()[0:1, 0:512],
                                                   scalar=1.0 / D, in1=st[0:1, 512:1024],
                                                   op0=ALU.mult, op1=ALU.subtract),
             reads=["ps1", "st_musq"], writes=["ps1", "st_var"])
        k.op("s", lambda e: e.activation(out=st[0:1, 1024:1536], in_=st[0:1, 1024:1536],
                                         func=AF.Ln, bias=P.cbias[0:1, 0:1]),
             reads=["st_var", "cbias"], writes=["st_var"])
        k.op("s", lambda e: e.activation(out=st[0:1, 1536:2048], in_=st[0:1, 1024:1536],
                                         func=AF.Exp, scale=-0.5),
             reads=["st_var"], writes=["st_rstd"])
        k.op("v", lambda e: e.tensor_tensor(out=P.mur2[0:1, :], in0=st[0:1, 0:512],
                                            in1=st[0:1, 1536:2048], op=ALU.mult),
             reads=["st_mu", "st_rstd"], writes=["mur2r0"])
        k.dma("y", P.rstd_hbm[0:1, s * 512:(s + 1) * 512],
              lambda kk: st[0:1, 1536:2048], reads=["st_rstd"])
        k.dma("y", P.rstd_hbm[1:2, s * 512:(s + 1) * 512],
              lambda kk: P.mur2[0:1, :], reads=["mur2r0"])
        for ts in range(NTS):
            k.ops("t", [
                (lambda e, c=c, ts=ts: e.transpose(
                    out=P.ps23.ap()[:, c * 128:(c + 1) * 128],
                    in_=bass.AP(P.zin[1], c * 512 + ts * 128, [[ZW, 128], [1, 128]]),
                    identity=P.identf[:]))
                for c in range(NCH)],
                reads=["y1", "identf"], writes=["ps23"])
            k.op("v", lambda e: e.tensor_copy(
                out=bass.AP(P.zin[0], 0, [[ZW, 128], [1, 1024]]),
                in_=P.ps23.ap()[:]),
                reads=["ps23"], writes=["ps23", "stin"])
            k.dma("y", P.ytok_hbm[(s * 4 + ts) * 128:(s * 4 + ts + 1) * 128, :],
                  lambda kk: bass.AP(P.zin[0], 0, [[ZW, 128], [1, 1024]]),
                  reads=["stin"])


def _mla_setup(k, P):
    for kc in range(NCH):
        k.dma("y", lambda kk, kc=kc: P.m_wq[:, kc * 1024:(kc + 1) * 1024],
              P.wq_b[kc, :, :], writes=["m_wq"], cls="dw")
        k.dma("y", lambda kk, kc=kc: P.m_wkv[:, kc * 128:(kc + 1) * 128],
              P.wkv_b[kc, :, :], writes=["m_wkv"], cls="dw")
    k.dma("y", lambda kk: P.m_wk[:], P.wk_b[:, :], writes=["m_wk"], cls="dw")
    k.dma("y", lambda kk: P.m_wv[:], P.wv_b[:, :], writes=["m_wv"], cls="dw")
    for g in range(4):
        k.dma("y", lambda kk, g=g: P.m_wo[:, g * 1024:(g + 1) * 1024],
              P.womla_b[g, :, :], writes=["m_wo"], cls="dw")
    k.dma("y", lambda kk: P.m_aug[:], P.wqaug_b[:, :], writes=["m_aug"], cls="dw")
    k.dma("y", lambda kk: P.m_augkv[:], P.wkvaug_b[:, :], writes=["m_augkv"], cls="dw")
    for name, gsrc, dst, dkey in (("q", P.aqg, P.m_qgrep, "m_qgrep"),
                                  ("k", P.akg, P.m_kgrep, "m_kgrep")):
        for h in range(16):
            k.dma("y", lambda kk, h=h: P.m_grow[0:1, h * 32:(h + 1) * 32],
                  gsrc[0:32].unsqueeze(0), writes=["m_grow"], cls="dw")
        k.op("v", lambda e: e.tensor_copy(out=P.m_growb[:], in_=P.m_grow[:]),
             reads=["m_grow"], writes=["m_growb"])
        k.op("t", lambda e: e.matmul(out=P.ps1.ap()[:, 0:512], lhsT=P.ones1[:, :],
                                     rhs=P.m_growb[0:1, :], start=True, stop=True),
             reads=["m_growb", "ones1"], writes=["ps1"])
        k.op("v", lambda e, dst=dst: e.tensor_copy(out=dst[:], in_=P.ps1.ap()[:, 0:512]),
             reads=["ps1"], writes=["ps1", dkey])


def _mla_stripe_setup(k, P, s):
    k.dma("y", lambda kk: bass.AP(P.m_y, 0, [[NCH * 512, 128], [512, NCH], [1, 512]]),
          bass.AP(P.y_hbm.tensor, 1 + s * 512,
                  [[ZCOLS, 128], [128 * ZCOLS, NCH], [1, 512]]),
          writes=["m_y"], cls="dz")
    k.dma("y", lambda kk: P.m_rstdf[:], P.rstd_hbm[0:1, s * 512:(s + 1) * 512],
          writes=["m_rstdf"], cls="dw")
    k.dma("y", lambda kk: P.m_murf[:], P.rstd_hbm[1:2, s * 512:(s + 1) * 512],
          writes=["m_murf"], cls="dw")
    k.op("v", lambda e: e.tensor_copy(out=P.m_rstd_b[:], in_=P.m_rstdf[:]),
         reads=["m_rstdf"], writes=["m_rstd_b"])
    k.op("v", lambda e: e.tensor_copy(out=P.m_mur_b[:], in_=P.m_murf[:]),
         reads=["m_murf"], writes=["m_mur_b"])
    k.op("t", lambda e: e.matmul(out=P.ps1.ap()[:, 0:512], lhsT=P.ones1[:, :],
                                 rhs=P.m_rstd_b[0:1, :], start=True, stop=True),
         reads=["m_rstd_b", "ones1"], writes=["ps1"])
    k.op("v", lambda e: e.tensor_copy(out=P.rstdb[:], in_=P.ps1.ap()[:, 0:512]),
         reads=["ps1"], writes=["ps1", "rstdb"])
    for c in range(NCH):
        k.op("p", lambda e, c=c: e.tensor_tensor(
            out=P.m_xa[:, c * 512:(c + 1) * 512], in0=P.m_y[:, c * 512:(c + 1) * 512],
            in1=P.rstdb[:], op=ALU.mult),
            reads=["m_y", "rstdb"], writes=["m_xa"])


def _mla_proj(k, P, s, ts):
    tt = 4 * s + ts
    for hf in range(2):
        def mk(kc, hf=hf, ts=ts):
            def f(e):
                return e.matmul(
                    out=P.ps23.ap()[:, hf * 512:(hf + 1) * 512],
                    lhsT=P.m_xa[:, kc * 512 + ts * 128: kc * 512 + ts * 128 + 128],
                    rhs=P.m_wq[:, kc * 1024 + hf * 512: kc * 1024 + hf * 512 + 512],
                    start=(kc == 0), stop=False)
            return f
        def mk_aug(hf=hf, ts=ts):
            def f(e):
                return e.matmul(
                    out=P.ps23.ap()[:, hf * 512:(hf + 1) * 512],
                    lhsT=P.m_mur_b[0:1, ts * 128:(ts + 1) * 128],
                    rhs=P.m_aug[0:1, hf * 512:(hf + 1) * 512],
                    start=False, stop=True)
            return f
        k.ops("t", [mk(kc) for kc in range(NCH)] + [mk_aug()],
              reads=["m_xa", "m_wq", "m_mur_b", "m_aug"], writes=["ps23"])
    for hf in range(2):
        k.op("s", lambda e, hf=hf: e.activation(
            out=P.m_sq[:, hf * 512:(hf + 1) * 512],
            in_=P.ps23.ap()[:, hf * 512:(hf + 1) * 512], func=AF.Square),
            reads=["ps23"], writes=["m_sq"])
    k.op("v", lambda e: e.tensor_reduce(
        out=P.m_r16[:], in_=P.m_sq.ap().rearrange("p (h d) -> p h d", d=64),
        axis=AX.X, op=ALU.add),
        reads=["m_sq"], writes=["m_r16"])
    k.op("s", lambda e: e.activation(out=P.m_r16[:], in_=P.m_r16[:], func=AF.Ln,
                                     scale=1.0 / 64, bias=P.cbias[:, 1:2]),
         reads=["m_r16", "cbias"], writes=["m_r16"])
    k.op("s", lambda e: e.activation(out=P.m_r16[:], in_=P.m_r16[:], func=AF.Exp,
                                     scale=-0.5),
         reads=["m_r16"], writes=["m_r16"])
    for h in range(16):
        k.op("v", lambda e, h=h: e.tensor_scalar(
            out=P.m_u[:, h * 32:(h + 1) * 32], in0=P.ps23.ap()[:, h * 64: h * 64 + 32],
            scalar1=P.m_r16[:, h:h + 1], scalar2=None, op0=ALU.mult),
            reads=["ps23", "m_r16"], writes=["m_u"])
    k.op("v", lambda e: e.tensor_tensor(out=P.m_sq[:, 0:512], in0=P.m_u[:],
                                        in1=P.m_qgrep[:], op=ALU.mult),
         reads=["m_u", "m_qgrep", "ps23"], writes=["ps23", "m_sq"])
    k.ops("t", [
        (lambda e, g=g: e.transpose(out=P.psb.ap()[:, g * 128:(g + 1) * 128],
                                    in_=P.m_sq[:, g * 128:(g + 1) * 128],
                                    identity=P.ident[:]))
        for g in range(4)],
        reads=["m_sq", "ident"], writes=["psb"])
    k.op("v", lambda e, ts=ts: e.tensor_copy(
        out=bass.AP(P.m_qhat, ts * 128, [[4 * 512, 128], [512, 4], [1, 128]]),
        in_=bass.AP(P.psb, 0, [[1024, 128], [128, 4], [1, 128]])),
        reads=["psb"], writes=["psb"] + [f"m_qh_{ts}"])
    def mkv(kc, ts=ts):
        def f(e):
            return e.matmul(
                out=P.ps0.ap()[:, 0:128],
                lhsT=P.m_xa[:, kc * 512 + ts * 128: kc * 512 + ts * 128 + 128],
                rhs=P.m_wkv[:, kc * 128:(kc + 1) * 128],
                start=(kc == 0), stop=False)
        return f
    def mkv_aug(ts=ts):
        def f(e):
            return e.matmul(out=P.ps0.ap()[:, 0:128],
                            lhsT=P.m_mur_b[0:1, ts * 128:(ts + 1) * 128],
                            rhs=P.m_augkv[0:1, :], start=False, stop=True)
        return f
    k.ops("t", [mkv(kc) for kc in range(NCH)] + [mkv_aug()],
          reads=["m_xa", "m_wkv", "m_mur_b", "m_augkv"], writes=["ps0"])
    k.op("v", lambda e: e.tensor_copy(out=P.m_sq[:, 512:640], in_=P.ps0.ap()[:, 0:128]),
         reads=["ps0"], writes=["ps0", "m_sq"])
    k.op("t", lambda e: e.transpose(out=P.psb.ap()[:, 512:640], in_=P.m_sq[:, 512:640],
                                    identity=P.ident[:]),
         reads=["m_sq", "ident"], writes=["psb"])
    k.op("v", lambda e, tt=tt: e.tensor_copy(
        out=bass.AP(P.m_kv, tt * 128, [[T, 128], [1, 128]]),
        in_=P.psb.ap()[:, 512:640]),
        reads=["psb"], writes=["psb", "m_kv"])
    k.op("t", lambda e, tt=tt: e.matmul(
        out=P.ps1.ap()[:, 0:512], lhsT=bass.AP(P.m_kv, tt * 128, [[T, 128], [1, 128]]),
        rhs=P.m_wk[:], start=True, stop=True),
        reads=["m_kv", "m_wk"], writes=["ps1"])
    k.op("s", lambda e: e.activation(out=P.m_sq[:, 0:512], in_=P.ps1.ap()[:],
                                     func=AF.Square),
         reads=["ps1"], writes=["m_sq"])
    k.op("v", lambda e: e.tensor_reduce(
        out=P.m_r16[:], in_=bass.AP(P.m_sq, 0, [[1024, 128], [32, 16], [1, 32]]),
        axis=AX.X, op=ALU.add),
        reads=["m_sq"], writes=["m_r16"])
    k.op("s", lambda e: e.activation(out=P.m_r16[:], in_=P.m_r16[:], func=AF.Ln,
                                     scale=1.0 / 32, bias=P.cbias[:, 1:2]),
         reads=["m_r16", "cbias"], writes=["m_r16"])
    k.op("s", lambda e: e.activation(out=P.m_r16[:], in_=P.m_r16[:], func=AF.Exp,
                                     scale=-0.5),
         reads=["m_r16"], writes=["m_r16"])
    for h in range(16):
        k.op("v", lambda e, h=h: e.tensor_scalar(
            out=P.m_u[:, h * 32:(h + 1) * 32], in0=P.ps1.ap()[:, h * 32:(h + 1) * 32],
            scalar1=P.m_r16[:, h:h + 1], scalar2=None, op0=ALU.mult),
            reads=["ps1", "m_r16"], writes=["m_u"])
    k.op("v", lambda e: e.tensor_tensor(out=P.m_sq[:, 512:1024], in0=P.m_u[:],
                                        in1=P.m_kgrep[:], op=ALU.mult),
         reads=["m_u", "m_kgrep", "ps1"], writes=["ps1", "m_sq"])
    k.ops("t", [
        (lambda e, g=g: e.transpose(out=P.psb.ap()[:, g * 128:(g + 1) * 128],
                                    in_=P.m_sq[:, 512 + g * 128: 512 + (g + 1) * 128],
                                    identity=P.ident[:]))
        for g in range(4)],
        reads=["m_sq", "ident"], writes=["psb"])
    k.op("v", lambda e, tt=tt: e.tensor_copy(
        out=bass.AP(P.m_khat, tt * 128, [[4 * T, 128], [T, 4], [1, 128]]),
        in_=bass.AP(P.psb, 0, [[1024, 128], [128, 4], [1, 128]])),
        reads=["psb"], writes=["psb", f"m_kh_{tt % 2}"])
    k.op("t", lambda e, tt=tt: e.matmul(
        out=P.ps1.ap()[:, 0:512], lhsT=bass.AP(P.m_kv, tt * 128, [[T, 128], [1, 128]]),
        rhs=P.m_wv[:], start=True, stop=True),
        reads=["m_kv", "m_wv"], writes=["ps1"])
    k.op("v", lambda e, tt=tt: e.tensor_copy(
        out=P.m_v[:, tt * 512:(tt + 1) * 512], in_=P.ps1.ap()[:]),
        reads=["ps1"], writes=["ps1", f"m_v_{tt % 2}"])


def _mla_unit(k, P, hp, qt):
    g = hp // 2
    rb = 64 * (hp % 2)
    nkeys = (qt + 1) * 128
    nc = (nkeys + 511) // 512
    qkeys = [f"m_qh_{qt % 4}"]
    khkeys = ["m_kh_0", "m_kh_1"]
    vkeys = ["m_v_0", "m_v_1"]

    def sim_mms(ci, ncols):
        def mk(jj, ci=ci, ncols=ncols):
            def f(e):
                return e.matmul(
                    out=P.ps23.ap()[:, jj * 512: jj * 512 + ncols],
                    lhsT=P.m_qhat[rb + jj * 32: rb + jj * 32 + 32,
                                  g * 512 + (qt % 4) * 128: g * 512 + (qt % 4) * 128 + 128],
                    rhs=P.m_khat[rb + jj * 32: rb + jj * 32 + 32,
                                 g * T + ci * 512: g * T + ci * 512 + ncols],
                    start=True, stop=True,
                    tile_position=(rb + jj * 32, 0))
            return f
        return [mk(0), mk(1)]

    def mask_diag(ci, ncols):
        db = (qt % 4) * 128
        k.op("v", lambda e, db=db: e.tensor_tensor(
            out=bass.AP(P.ps23, db, [[1024, 128], [512, 2], [1, 128]]),
            in0=bass.AP(P.ps23, db, [[1024, 128], [512, 2], [1, 128]]),
            in1=bass.AP(P.maskL2, 0, [[256, 128], [128, 2], [1, 128]]),
            op=ALU.add),
            reads=["ps23", "maskL"], writes=["ps23"])

    for ci in range(nc):
        ncols = min(512, nkeys - ci * 512)
        k.ops("t", sim_mms(ci, ncols), reads=qkeys + khkeys, writes=["ps23"])
        if ci == nc - 1:
            mask_diag(ci, ncols)
        k.op("v", lambda e, ncols=ncols: e.tensor_reduce(
            out=P.m_cmax[:], in_=bass.AP(P.ps23, 0, [[1024, 128], [512, 2], [1, ncols]]),
            axis=AX.X, op=ALU.max),
            reads=["ps23"], writes=["ps23", "m_cmax"])
        if ci == 0:
            k.op("v", lambda e: e.tensor_copy(out=P.m_negm[:], in_=P.m_cmax[:]),
                 reads=["m_cmax"], writes=["m_negm"])
        else:
            k.op("v", lambda e: e.tensor_tensor(out=P.m_negm[:], in0=P.m_negm[:],
                                                in1=P.m_cmax[:], op=ALU.max),
                 reads=["m_cmax", "m_negm"], writes=["m_negm"])
    k.op("v", lambda e: e.tensor_scalar(out=P.m_nb[:], in0=P.m_negm[:],
                                        scalar1=-0.125, scalar2=None, op0=ALU.mult),
         reads=["m_negm"], writes=["m_nb"])
    for ci in range(nc):
        ncols = min(512, nkeys - ci * 512)
        k.ops("t", sim_mms(ci, ncols), reads=qkeys + khkeys, writes=["ps23"])
        if ci == nc - 1:
            mask_diag(ci, ncols)
        acc = P.m_sums if ci == 0 else P.m_cmax
        akey = "m_sums" if ci == 0 else "m_cmax"
        for jj in range(2):
            k.op("s", lambda e, jj=jj, ci=ci, ncols=ncols, acc=acc: e.activation(
                out=P.m_P[:, jj * T + ci * 512: jj * T + ci * 512 + ncols],
                in_=P.ps23.ap()[:, jj * 512: jj * 512 + ncols],
                func=AF.Exp, scale=0.125, bias=P.m_nb[:, jj:jj + 1],
                accum_out=acc[:, jj:jj + 1]),
                reads=["ps23", "m_nb"], writes=["ps23", "m_P", akey])
        if ci > 0:
            k.op("v", lambda e: e.tensor_tensor(out=P.m_sums[:], in0=P.m_sums[:],
                                                in1=P.m_cmax[:], op=ALU.add),
                 reads=["m_cmax", "m_sums"], writes=["m_sums"])
    k.op("v", lambda e: e.reciprocal(out=P.m_rsum[:], in_=P.m_sums[:]),
         reads=["m_sums"], writes=["m_rsum"])
    for jj in range(2):
        k.op("v", lambda e, jj=jj: e.tensor_scalar(
            out=P.m_diag2[:, jj * 128:(jj + 1) * 128], in0=P.ident[:],
            scalar1=P.m_rsum[:, jj:jj + 1], scalar2=None, op0=ALU.mult),
            reads=["m_rsum", "ident"], writes=["m_diag2"])
    nb_total = qt + 1
    first_pv = (hp % 2 == 0)
    for w in range(nc):
        blocks = list(range(w * 4, min(w * 4 + 4, nb_total)))
        def mk_pt(jj, b, w=w):
            def f(e):
                return e.matmul(
                    out=P.ps45.ap()[:, jj * 512 + (b % 4) * 128: jj * 512 + (b % 4) * 128 + 128],
                    lhsT=P.m_P[:, jj * T + b * 128: jj * T + b * 128 + 128],
                    rhs=P.m_diag2[:, jj * 128:(jj + 1) * 128],
                    start=True, stop=True)
            return f
        for jj in range(2):
            k.ops("t", [mk_pt(jj, b) for b in blocks],
                  reads=["m_P", "m_diag2"], writes=["ps45"])
        nbw = len(blocks)
        k.op("v", lambda e, nbw=nbw: e.tensor_copy(
            out=bass.AP(P.m_Pt, 0, [[2048, 128], [1024, 2], [1, nbw * 128]]),
            in_=bass.AP(P.ps45, 0, [[1024, 128], [512, 2], [1, nbw * 128]])),
            reads=["ps45"], writes=["ps45", "m_Pt"])
        mms = []
        for jj in range(2):
            h = 4 * g + 2 * (hp % 2) + jj
            for b in blocks:
                is_first = (w == 0) and (jj == 0) and (b == blocks[0])
                is_last = (w == nc - 1) and (jj == 1) and (b == blocks[-1])
                def f(e, jj=jj, b=b, h=h, is_first=is_first, is_last=is_last):
                    return e.matmul(
                        out=P.ps6.ap()[0:32, jj * 128:(jj + 1) * 128],
                        lhsT=P.m_v[:, b * 512 + h * 32: b * 512 + h * 32 + 32],
                        rhs=P.m_Pt[:, jj * 1024 + (b % 4) * 128: jj * 1024 + (b % 4) * 128 + 128],
                        start=is_first, stop=is_last)
                mms.append(f)
        k.ops("t", mms, reads=["m_Pt"] + vkeys, writes=["ps6"])
    for jj in range(2):
        k.op("v", lambda e, jj=jj: e.tensor_copy(
            out=P.m_ao[rb + jj * 32: rb + jj * 32 + 32, g * 128:(g + 1) * 128],
            in_=P.ps6.ap()[0:32, jj * 128:(jj + 1) * 128]),
            reads=["ps6"], writes=["ps6", f"m_ao_{g}_{hp % 2}"])


def _mla_wo_resid(k, P, qt):
    aokeys = [f"m_ao_{g}_{h}" for g in range(4) for h in range(2)]
    k.dma("y", lambda kk: P.m_ytok[:], P.ytok_hbm[qt * 128:(qt + 1) * 128, :],
          writes=["m_ytok"], cls="dz")
    for half in range(2):
        def mk(g, half=half):
            def f(e):
                return e.matmul(
                    out=P.ps45.ap()[:, half * 512:(half + 1) * 512],
                    lhsT=P.m_ao[:, g * 128:(g + 1) * 128],
                    rhs=P.m_wo[:, g * 1024 + half * 512: g * 1024 + half * 512 + 512],
                    start=(g == 0), stop=(g == 3))
            return f
        k.ops("t", [mk(g) for g in range(4)],
              reads=aokeys + ["m_wo"], writes=["ps45"])
    k.op("v", lambda e: e.tensor_tensor(out=P.m_y2[:], in0=P.ps45.ap()[:],
                                        in1=P.m_ytok[:], op=ALU.add),
         reads=["ps45", "m_ytok"], writes=["ps45", "m_y2"])
    k.dma("y", P.y2tok_hbm[qt * 128:(qt + 1) * 128, :], lambda kk: P.m_y2[:],
          reads=["m_y2"])


def emit_mla(k, P):
    _mla_setup(k, P)
    for s in range(NS):
        _mla_stripe_setup(k, P, s)
        for ts in range(NTS):
            _mla_proj(k, P, s, ts)
        if TAP_MLA[0] and s == 0:
            pkeys = [f"m_qh_{t}" for t in range(4)] + ["m_kh_0", "m_kh_1",
                     "m_v_0", "m_v_1", "m_kv", "m_xa"]
            k.dma("y", bass.AP(P.tap_hbm.tensor, 0, [[57344, 128], [1, 2048]]),
                  lambda kk: P.m_qhat[:], reads=pkeys)
            k.dma("y", bass.AP(P.tap_hbm.tensor, 2048, [[57344, 128], [1, 2048]]),
                  lambda kk: bass.AP(P.m_khat, 0, [[4 * T, 128], [T, 4], [1, 512]]),
                  reads=pkeys)
            k.dma("y", bass.AP(P.tap_hbm.tensor, 4096, [[57344, 128], [1, 2048]]),
                  lambda kk: P.m_v[:, 0:2048], reads=pkeys)
            k.dma("y", bass.AP(P.tap_hbm.tensor, 6144, [[57344, 128], [1, 512]]),
                  lambda kk: bass.AP(P.m_kv, 0, [[T, 128], [1, 512]]), reads=pkeys)
            k.dma("y", bass.AP(P.tap_hbm.tensor, 6656, [[57344, 128], [1, 4096]]),
                  lambda kk: P.m_xa[:], reads=pkeys)
        for ts in range(NTS):
            qt = 4 * s + ts
            for hp in range(8):
                _mla_unit(k, P, hp, qt)
                if TAP_MLA[0] and qt == TAPQT[0] and hp == 0:
                    k.dma("y", bass.AP(P.tap_hbm.tensor, 10752, [[57344, 128], [1, 512]]),
                          lambda kk: P.m_P[:, 0:512], reads=["m_P"])
                    k.dma("y", bass.AP(P.tap_hbm.tensor, 11264, [[57344, 128], [1, 256]]),
                          lambda kk: bass.AP(P.m_P, T, [[2 * T, 128], [1, 256]]),
                          reads=["m_P"])
                    k.dma("y", bass.AP(P.tap_hbm.tensor, 11520, [[57344, 128], [1, 4]]),
                          lambda kk: P.m_negm[:].bitcast(BF16), reads=["m_negm"])
                    k.dma("y", bass.AP(P.tap_hbm.tensor, 11524, [[57344, 128], [1, 4]]),
                          lambda kk: P.m_sums[:].bitcast(BF16), reads=["m_sums"])
                    k.dma("y", bass.AP(P.tap_hbm.tensor, 11528, [[57344, 128], [1, 512]]),
                          lambda kk: P.m_ao[:], reads=[f"m_ao_0_0"])
                    k.dma("y", bass.AP(P.tap_hbm.tensor, 12040, [[57344, 128], [1, 2048]]),
                          lambda kk: P.m_Pt[:], reads=["m_Pt", f"m_ao_0_0"])
                    k.dma("y", bass.AP(P.tap_hbm.tensor, 14088, [[57344, 128], [1, 256]]),
                          lambda kk: P.m_diag2[:], reads=["m_diag2", f"m_ao_0_0"])
                    k.dma("y", bass.AP(P.tap_hbm.tensor, 14344, [[57344, 128], [1, 4]]),
                          lambda kk: P.m_rsum[:].bitcast(BF16), reads=["m_rsum", f"m_ao_0_0"])
            _mla_wo_resid(k, P, qt)


def emit_stage_z2(k, P):
    emit_stage_in(k, P, src=P.y2tok_hbm, write_x=False)


def emit_stage_out(k, P):
    for tt in range(32):
        k.dma("y", lambda kk: bass.AP(P.zin[0], 0, [[ZW, 128], [1, 1024]]),
              P.y2tok_hbm[tt * 128:(tt + 1) * 128, :], writes=["so_y"], cls="dz")
        k.dma("y", lambda kk: bass.AP(P.zin[0], 1040, [[ZW, 128], [128, NCH], [1, 128]]),
              bass.AP(P.z_hbm.tensor, 1 + tt * 128,
                      [[ZCOLS, 128], [128 * ZCOLS, NCH], [1, 128]]),
              writes=["so_z"], cls="dw")
        k.ops("t", [
            (lambda e, c=c: e.transpose(
                out=P.ps23.ap()[:, c * 128:(c + 1) * 128],
                in_=bass.AP(P.zin[0], 1040 + c * 128, [[ZW, 128], [1, 128]]),
                identity=P.identf[:]))
            for c in range(NCH)],
            reads=["so_z", "identf"], writes=["ps23"])
        k.op("v", lambda e: e.scalar_tensor_tensor(
            out=bass.AP(P.zin[0], 2080, [[ZW, 128], [1, 1024]]),
            in0=P.ps23.ap()[:], scalar=0.5,
            in1=bass.AP(P.zin[0], 0, [[ZW, 128], [1, 1024]]),
            op0=ALU.mult, op1=ALU.add),
            reads=["ps23", "so_y"], writes=["ps23", "so_out"])
        k.dma("y", P.y_out[tt * 128:(tt + 1) * 128, :],
              lambda kk: bass.AP(P.zin[0], 2080, [[ZW, 128], [1, 1024]]),
              reads=["so_out"])


def make_in_map(inputs, core):
    tabs = host_tables()
    m = {}
    for name, arr in inputs.items():
        if name == "L0_attn_x":
            m[name] = _f32(arr[core])
        else:
            m[name] = _f32(arr)
    m["tab_c1"] = tabs["c1"]
    m["tab_c2s"] = tabs["c2s"]
    m["tab_c3"] = tabs["c3"]
    m["tab_c4s"] = tabs["c4s"]
    return m


def kernel(**inputs):
    nc, P = build(parts=("precast", "stagein", "flayers1", "mla", "flayers2", "stageout"))
    in_maps = [make_in_map(inputs, c) for c in range(8)]
    res = run_bass_kernel_spmd(nc, in_maps, core_ids=list(range(8)))
    out = np.stack([r["out"] for r in res.results])
    return out.astype(np.float32)
